# version 27
# baseline (speedup 1.0000x reference)
import numpy as np
import ml_dtypes
from contextlib import ExitStack

import concourse.bass as bass
import concourse.mybir as mybir
from concourse.bass import IndirectOffsetOnAxis
from concourse.bass_utils import run_bass_kernel_spmd

F32 = mybir.dt.float32
BF16 = mybir.dt.bfloat16
I32 = mybir.dt.int32
AF = mybir.ActivationFunctionType
ALU = mybir.AluOpType
AX = mybir.AxisListType.X

TOK = 4096
DM = 1024
HID = 2048
NE = 16
NH, HD = 16, 64
NB = TOK // 128
NTT = TOK // 512
SCALE = 0.125
CAP = 1024
CS = 1280
NROW = NE * CS


class Builder:
    ENG2SEM = {"p": "sp", "a": "sa", "v": "sv", "g": "sg", "s": "ss"}

    def __init__(self):
        self.items = []
        self.cnt = {}
        self.wm = {}
        self.reg = {}
        self.pend = {}
        self.sem_names = set(self.ENG2SEM.values())
        self.dma_sems = set()

    def op(self, eng, fn, reads=(), writes=(), sem="ENG", amt=1, tag=""):
        waits = {}

        def need(m):
            if m is None:
                return
            s, v, meng = m
            if meng == "p" and eng == "p":
                return
            assert v is not None, f"unresolved marker ({s}) needed by {tag}"
            if s in self.dma_sems:
                v = self.cnt[s]
            if waits.get(s, 0) < v:
                waits[s] = v

        for k in reads:
            st = self.reg.get(k)
            if st:
                need(st["w"])
        for k in writes:
            st = self.reg.get(k)
            if st:
                need(st["w"])
                for m in st["r"]:
                    need(m)
        wmm = self.wm.setdefault(eng, {})
        wlist = []
        for s, v in sorted(waits.items()):
            if wmm.get(s, 0) < v:
                wlist.append((s, v))
                wmm[s] = v
        if sem == "ENG":
            sem = self.ENG2SEM[eng]
        if sem is not None:
            self.sem_names.add(sem)
            if amt != 1:
                self.dma_sems.add(sem)
            self.cnt[sem] = self.cnt.get(sem, 0) + amt
            newv = self.cnt[sem]
            for m in self.pend.pop(sem, []):
                m[1] = newv
            marker = [sem, newv, eng]
        else:
            esem = self.ENG2SEM[eng]
            marker = [esem, None, eng]
            self.pend.setdefault(esem, []).append(marker)
        self.items.append((eng, wlist, fn, sem, amt, tag))
        for k in writes:
            self.reg[k] = {"w": marker, "r": []}
        for k in reads:
            self.reg.setdefault(k, {"w": None, "r": []})["r"].append(marker)

    def finish(self):
        leftover = {k: v for k, v in self.pend.items() if v}
        assert not leftover, f"pending markers never resolved: {list(leftover)}"


def build_nc():
    nc = bass.Bass("TRN2", target_bir_lowering=False)

    x_d = nc.dram_tensor("x", [TOK, DM], F32, kind="ExternalInput")
    wq_d = nc.dram_tensor("wq", [DM, DM], F32, kind="ExternalInput")
    wk_d = nc.dram_tensor("wk", [DM, DM], F32, kind="ExternalInput")
    wv_d = nc.dram_tensor("wv", [DM, DM], F32, kind="ExternalInput")
    wo_d = nc.dram_tensor("wo", [DM, DM], F32, kind="ExternalInput")
    bq_d = nc.dram_tensor("bq", [DM], F32, kind="ExternalInput")
    bk_d = nc.dram_tensor("bk", [DM], F32, kind="ExternalInput")
    bv_d = nc.dram_tensor("bv", [DM], F32, kind="ExternalInput")
    bo_d = nc.dram_tensor("bo", [DM], F32, kind="ExternalInput")
    gate_d = nc.dram_tensor("gate", [DM, NE], F32, kind="ExternalInput")
    w1_d = nc.dram_tensor("w1", [NE, DM, HID], F32, kind="ExternalInput")
    b1_d = nc.dram_tensor("b1", [NE, HID], F32, kind="ExternalInput")
    w2_d = nc.dram_tensor("w2", [NE, HID, DM], F32, kind="ExternalInput")
    b2_d = nc.dram_tensor("b2", [NE, DM], F32, kind="ExternalInput")
    y_d = nc.dram_tensor("y", [TOK, DM], F32, kind="ExternalOutput")

    xe_d = nc.dram_tensor("xed", [NROW, DM], BF16, kind="Internal")
    oe_d = nc.dram_tensor("oed", [NROW, DM], BF16, kind="Internal")
    cnt_d = nc.dram_tensor("cntd", [NB, NE], F32, kind="Internal")

    ident_d = nc.inline_tensor(np.eye(128, dtype=ml_dtypes.bfloat16), "identc")
    ones_d = nc.inline_tensor(np.ones((1, 512), dtype=ml_dtypes.bfloat16), "onesc")
    lt_np = np.triu(np.ones((128, 128), dtype=np.float32), 1).astype(
        ml_dtypes.bfloat16)
    lt_d = nc.inline_tensor(np.ascontiguousarray(lt_np), "ltc")
    onescol_d = nc.inline_tensor(np.ones((128, 1), dtype=ml_dtypes.bfloat16), "occ")
    identf_d = nc.inline_tensor(np.eye(16, dtype=np.float32), "idfc")
    ecols_d = nc.inline_tensor(
        np.broadcast_to(np.arange(16, dtype=np.float32), (128, 16)).copy(), "ecc")

    B = Builder()

    with ExitStack() as ctx:
        e = ctx.enter_context
        xnf = e(nc.sbuf_tensor("xnf", [128, 8 * TOK], BF16))
        wbuf = e(nc.sbuf_tensor("wbuf", [128, 32768], BF16))
        xef = e(nc.sbuf_tensor("xef", [128, 8192], BF16))
        hidq = e(nc.sbuf_tensor("hidq", [128, 8 * 512], BF16))
        kv = e(nc.sbuf_tensor("kv", [128, 2048], BF16))
        kv2 = e(nc.sbuf_tensor("kv2", [128, 2048], BF16))
        kvt = lambda pp: (kv if pp < 2 else kv2, (pp % 2) * 1024)
        xin = e(nc.sbuf_tensor("xin", [128, 2048], F32))
        outt = e(nc.sbuf_tensor("outt", [128, 2048], F32))
        silb = e(nc.sbuf_tensor("silb", [128, 2048], BF16))
        outsb = silb
        acct = e(nc.sbuf_tensor("acct", [128, 2048], F32))
        spsb = e(nc.sbuf_tensor("spsb", [128, 512], BF16))
        gall = e(nc.sbuf_tensor("gall", [128, NB * NE], F32))
        lgall = e(nc.sbuf_tensor("lgall", [128, NB * NE], F32))
        mkall = e(nc.sbuf_tensor("mkall", [128, NB * NE], BF16))
        dsti = e(nc.sbuf_tensor("dsti", [128, NB * 3], I32))
        gsl = e(nc.sbuf_tensor("gsl", [128, NB * 3], F32))
        postm = e(nc.sbuf_tensor("postm", [128, 64], F32))
        stats = e(nc.sbuf_tensor("stats", [128, 148], F32))
        crow = e(nc.sbuf_tensor("crow", [1, 2048], BF16))
        b2row = e(nc.sbuf_tensor("b2row", [1, DM], BF16))
        b1col = e(nc.sbuf_tensor("b1col", [128, 256], F32))
        b1stg = e(nc.sbuf_tensor("b1stg", [16, 256], F32))
        identf = e(nc.sbuf_tensor("identf", [16, 16], F32))
        gate_s = e(nc.sbuf_tensor("gate_s", [128, 128], BF16))
        ident_s = e(nc.sbuf_tensor("ident_s", [128, 128], BF16))
        lt_s = e(nc.sbuf_tensor("lt_s", [128, 128], BF16))
        ecol_s = e(nc.sbuf_tensor("ecol_s", [128, 16], F32))
        onescol_s = e(nc.sbuf_tensor("onescol_s", [128, 1], BF16))
        ones_s = e(nc.sbuf_tensor("ones_s", [1, 512], BF16))

        ps = [e(nc.psum_tensor(f"ps{i}", [128, 512], F32)) for i in range(8)]
        ps6b = ps[6][:, :].bitcast(BF16)
        ps7b = ps[7][:, :].bitcast(BF16)

        sems = {}
        for name in ["sp", "sa", "sv", "sg", "ss",
                     "dwA", "dwB", "dxe", "dbc0", "dbc1", "db2", "dcst",
                     "dcr", "dsc", "dg0", "dg1", "dh0", "dh1", "dz",
                     "dx0", "dx1", "dy0", "dy1"]:
            sems[name] = e(nc.semaphore(name))

        def dma(queue_eng, sem, out_ap, in_ap, reads, writes, tag):
            B.op(queue_eng,
                 (lambda o, i: (lambda eng: eng.dma_start(o, i)))(out_ap, in_ap),
                 reads=reads, writes=writes, sem=sem, amt=16, tag=tag)

        def wc(col0, ncols):
            return wbuf[:, col0:col0 + ncols]

        def emit_xe_blk(ex, blk):
            bp = blk % 4
            tt_, c0 = kvt(bp)
            dma("s", "dxe", tt_[:, c0:c0 + 1024],
                xe_d[ex * CS + blk * 128: ex * CS + (blk + 1) * 128, :],
                ("xed",), (f"kvp{bp}",), f"xe_{ex}_{blk}")

        def emit_xe(ex):
            for blk in range(4):
                emit_xe_blk(ex, blk)


        steps = [("A",)]
        for ex in range(NE):
            for hh in range(2):
                steps.append(("B", ex, hh))
        steps.append(("C",))

        def slot_base(si):
            return (si % 2) * 16384

        def slot_key(si):
            return "sA" if si % 2 == 0 else "sB"

        def emit_loads(si):
            st = steps[si]
            base = slot_base(si)
            sk = slot_key(si)
            dwS = "dwA" if si % 2 == 0 else "dwB"
            if st[0] == "A":
                for h_ in range(2):
                    dma("g", dwS, wc(base + h_ * 4096, 4096),
                        wk_d[h_ * 512:(h_ + 1) * 512, :].rearrange(
                            "(c p) d -> p c d", p=128),
                        (), (sk,), f"ld_wk{h_}")
                for h_ in range(2):
                    dma("g", dwS, wc(base + 8192 + h_ * 4096, 4096),
                        wv_d[h_ * 512:(h_ + 1) * 512, :].rearrange(
                            "(c p) d -> p c d", p=128),
                        (), (sk,), f"ld_wv{h_}")
                dma("g", "dcr", crow[0:1, 0:1024], bk_d[None, :], (), ("crow",), "ld_bk")
                dma("g", "dcr", crow[0:1, 1024:2048], bv_d[None, :], (), ("crow",), "ld_bv")
            elif st[0] == "B":
                _, ex, hh = st
                dma("g", dwS, wc(base, 8192),
                    w1_d[ex, :, hh * 1024:(hh + 1) * 1024].rearrange(
                        "(c p) h -> p c h", p=128),
                    (), (sk,), f"ld_w1_{ex}_{hh}")
                dma("g", dwS, wc(base + 8192, 8192),
                    w2_d[ex, hh * 1024:(hh + 1) * 1024, :].rearrange(
                        "(j p) d -> p j d", p=128),
                    (), (sk,), f"ld_w2_{ex}_{hh}")
                if hh == 0:
                    dma("g", "db2", b2row[0:1, :], b2_d[ex:ex + 1, :],
                        (), ("b2row",), f"ld_b2_{ex}")
            else:
                dma("g", dwS, wc(base, 8192),
                    wq_d[:, :].rearrange("(c p) d -> p c d", p=128),
                    (), (sk,), "ld_wq")
                dma("g", dwS, wc(base + 8192, 8192),
                    wo_d[:, :].rearrange("(c p) d -> p c d", p=128),
                    (), (sk,), "ld_wo")
                dma("g", "dcr", crow[0:1, 0:1024], bq_d[None, :], (), ("crow",), "ld_bq")
                dma("g", "dcr", crow[0:1, 1024:2048], bo_d[None, :], (), ("crow",), "ld_bo")

        emit_loads(0)
        emit_loads(1)
        dma("g", "dcst", ident_s[:, :], ident_d[:, :], (), ("ident",), "ld_ident")
        dma("g", "dcst", ones_s[0:1, :], ones_d[:, :], (), ("ones",), "ld_ones")
        dma("g", "dcst", lt_s[:, :], lt_d[:, :], (), ("lt",), "ld_lt")
        dma("g", "dcst", onescol_s[:, :], onescol_d[:, :], (), ("oc",), "ld_oc")
        dma("g", "dcst", ecol_s[:, :], ecols_d[:, :], (), ("ec",), "ld_ec")
        dma("s", "dcr", identf[:, :], identf_d[:, :], (), ("idf",), "ld_idf")
        for c in range(8):
            dma("g", "dcst", gate_s[:, c * 16:(c + 1) * 16],
                gate_d[c * 128:(c + 1) * 128, :], (), ("gate",), f"ld_gate{c}")
        for i in range(2):
            dma("s", f"dx{i % 2}", xin[:, (i % 2) * 1024:(i % 2) * 1024 + 1024],
                x_d[i * 128:(i + 1) * 128, :], (), (f"xin{i % 2}",), f"ld_x{i}")
        for ex_ in range(NE):
            ep_ = ex_ % 2
            dma("s", f"dbc{ep_}", b1stg[0:16, ep_ * 128:(ep_ + 1) * 128],
                b1_d[ex_, :].rearrange("(m p) -> m p", p=128),
                (), (f"b1s{ep_}",), f"ld_b1_{ex_}")
            B.op("p", (lambda e_: (lambda eng: eng.transpose(
                     ps[6][:, 0:16], b1stg[0:16, (e_ % 2) * 128:(e_ % 2 + 1) * 128],
                     identf[:, :])))(ex_),
                 reads=(f"b1s{ep_}", "idf"), writes=("ps6",), sem="sp",
                 tag=f"b1t_{ex_}")
            B.op("a", (lambda e_: (lambda eng: eng.activation(
                     b1col[:, e_ * 16:(e_ + 1) * 16], ps[6][:, 0:16],
                     AF.Copy)))(ex_),
                 reads=("ps6",), writes=("b1c",), tag=f"b1e_{ex_}")

        B.op("v", lambda eng: eng.memset(silb[:, :], 0.0),
             reads=(), writes=("silb0", "silb1"), tag="zero_silb")
        B.op("v", lambda eng: eng.memset(stats[:, 144:145], -0.5),
             reads=(), writes=("stats_c",), tag="c_half")
        B.op("v", lambda eng: eng.memset(stats[:, 145:146], 1e-5),
             reads=(), writes=("stats_e",), tag="c_eps")
        for e_ in range(NE):
            for hq in range(2):
                dma("s", "dz", oe_d[e_ * CS + CAP + hq * 128: e_ * CS + CAP + (hq + 1) * 128, :],
                    silb[:, 0:1024], ("silb0",), ("odz",), f"zero_oed{e_}_{hq}")

        B.op("p",
             lambda eng: eng.matmul(
                 ps[4][:, :], ones_s[0:1, 0:128],
                 silb[0:1, 0:512], start=True, stop=False,
                 skip_group_check=True),
             reads=("ones", "silb0"), writes=("ps4",), sem="sp", tag="zmm4")

        def emit_A1(si):
            def ln_part(i):
                p = i % 2
                xk = f"xin{p}"
                xtile = xin[:, p * 1024:p * 1024 + 1024]
                so = 72 * p
                stk = lambda n: f"st{n}_{p}"
                sl = silb[:, p * 1024:(p + 1) * 1024]
                slk = f"silb{p}"
                B.op("v", (lambda xt, o: (lambda eng: eng.reduce_sum(
                         stats[:, o:o + 1], xt, axis=AX)))(xtile, so),
                     reads=(xk,), writes=(stk(0),), tag=f"ln_sum{i}")
                B.op("v", (lambda xt, o, pp: (lambda eng: eng.scalar_tensor_tensor(
                         outt[:, pp * 1024:pp * 1024 + 1024], xt, 0.0, xt,
                         op0=ALU.add, op1=ALU.mult,
                         accum_out=stats[:, o + 1:o + 2])))(xtile, so, p),
                     reads=(xk,), writes=(f"out{p}", stk(1)), tag=f"ln_sq{i}")
                B.op("v", (lambda o: (lambda eng: eng.tensor_scalar(
                         stats[:, o + 2:o + 3], stats[:, o:o + 1],
                         1.0 / DM, None, op0=ALU.mult)))(so),
                     reads=(stk(0),), writes=(stk(2),), tag=f"ln_mean{i}")
                B.op("v", (lambda o: (lambda eng: eng.tensor_scalar(
                         stats[:, o + 3:o + 4], stats[:, o + 2:o + 3],
                         stats[:, o + 2:o + 3], -1.0,
                         op0=ALU.mult, op1=ALU.mult)))(so),
                     reads=(stk(2),), writes=(stk(3),), tag=f"ln_nmsq{i}")
                B.op("v", (lambda o: (lambda eng: eng.tensor_scalar(
                         stats[:, o + 4:o + 5], stats[:, o + 1:o + 2],
                         1.0 / DM, stats[:, o + 3:o + 4],
                         op0=ALU.mult, op1=ALU.add)))(so),
                     reads=(stk(1), stk(3)), writes=(stk(4),), tag=f"ln_var{i}")
                B.op("a", (lambda o: (lambda eng: eng.activation(
                         stats[:, o + 5:o + 6], stats[:, o + 4:o + 5], AF.Ln,
                         bias=stats[:, 145:146])))(so),
                     reads=(stk(4), "stats_e"), writes=(stk(5),), tag=f"ln_ln{i}")
                B.op("a", (lambda o: (lambda eng: eng.activation(
                         stats[:, o + 6:o + 7], stats[:, o + 5:o + 6], AF.Exp,
                         scale=stats[:, 144:145])))(so),
                     reads=(stk(5), "stats_c"), writes=(stk(6),), tag=f"ln_exp{i}")
                B.op("v", (lambda xt, o, s_: (lambda eng: eng.tensor_scalar(
                         s_, xt, stats[:, o + 2:o + 3], stats[:, o + 6:o + 7],
                         op0=ALU.subtract, op1=ALU.mult)))(xtile, so, sl),
                     reads=(xk, stk(2), stk(6)), writes=(slk,), tag=f"ln_xn{i}")
                for c in range(8):
                    bank = 7 if c % 2 == 0 else 6
                    bview = (ps7b if bank == 7 else ps6b)[:, 0:128]
                    B.op("p", (lambda bv, cc, s_: (lambda eng: eng.transpose(
                             bv, s_[:, cc * 128:(cc + 1) * 128], ident_s[:, :])))(bview, c, sl),
                         reads=(slk, "ident"), writes=(f"ps{bank}",), sem="sp",
                         tag=f"tp{i}_{c}")
                    B.op("a", (lambda bv, cc, ii: (lambda eng: eng.activation(
                             xnf[:, cc * TOK + ii * 128: cc * TOK + ii * 128 + 128],
                             bv, AF.Copy)))(bview, c, i),
                         reads=(f"ps{bank}",), writes=(f"xnf{i}",), tag=f"tpe{i}_{c}")
                if i + 2 < NB:
                    j = i + 2
                    dma("s", f"dx{j % 2}", xin[:, (j % 2) * 1024:(j % 2) * 1024 + 1024],
                        x_d[j * 128:(j + 1) * 128, :], (), (f"xin{j % 2}",), f"ld_x{j}")

            def logits_part(i):
                for c in range(8):
                    B.op("p", (lambda cc, ii: (lambda eng: eng.matmul(
                             ps[5][:, 0:16],
                             xnf[:, cc * TOK + ii * 128: cc * TOK + ii * 128 + 128],
                             gate_s[:, cc * 16:(cc + 1) * 16],
                             start=(cc == 0), stop=(cc == 7))))(c, i),
                         reads=(f"xnf{i}", "gate"), writes=("ps5l",),
                         sem=("sp" if c == 7 else None), tag=f"lg{i}_{c}")

            def gate_part(i):
                p = i % 2
                so = 72 * p
                B.op("v", (lambda ii: (lambda eng: eng.tensor_copy(
                         lgall[:, ii * 16:(ii + 1) * 16], ps[5][:, 0:16])))(i),
                     reads=("ps5l",), writes=("lga",), tag=f"lge{i}")
                B.op("v", (lambda o, ii: (lambda eng: eng.max(
                         out=stats[:, o + 8:o + 16],
                         in_=lgall[:, ii * 16:(ii + 1) * 16])))(so, i),
                     reads=("lga",), writes=(f"st_m8{p}",), tag=f"g_max{i}")
                B.op("v", (lambda o, ii: (lambda eng: eng.tensor_scalar(
                         mkall[:, ii * 16:(ii + 1) * 16], lgall[:, ii * 16:(ii + 1) * 16],
                         stats[:, o + 10:o + 11], None, op0=ALU.is_ge)))(so, i),
                     reads=("lga", f"st_m8{p}"), writes=("mka",), tag=f"g_mask{i}")
                B.op("p", (lambda ii: (lambda eng: eng.matmul(
                         ps[5][0:16, 16 + ii:17 + ii],
                         mkall[:, ii * 16:(ii + 1) * 16], onescol_s[:, 0:1],
                         start=True, stop=True)))(i),
                     reads=("mka", "oc"), writes=(f"ps5c{i % 4}",),
                     sem="sp", tag=f"cnt{i}")

            for i in range(NB):
                ln_part(i)
                if i >= 1:
                    gate_part(i - 1)
                logits_part(i)
            gate_part(NB - 1)

        def emit_base():
            B.op("v", lambda eng: eng.tensor_copy(kv[0:16, 0:32],
                                                  ps[5][0:16, 16:48]),
                 reads=("ps5c0", "ps5c1", "ps5c2", "ps5c3"), writes=("kvp0",),
                 tag="cnt_ev")
            B.op("p", lambda eng: eng.transpose(
                     ps7b[0:32, 0:16], kv[0:16, 0:32], ident_s[0:16, 0:16]),
                 reads=("kvp0", "ident"), writes=("ps7",), sem="sp", tag="cnt_tp")
            B.op("v", lambda eng: eng.tensor_copy(postm[0:32, 0:16], ps7b[0:32, 0:16]),
                 reads=("ps7",), writes=("postm",), tag="cnt_tpe")
            dma("s", "dsc", cnt_d[:, :], postm[0:32, 0:16], ("postm",), ("cntd",),
                "cnt_st")
            baser = outt
            baseb = silb
            B.op("v", lambda eng: eng.memset(baser[0:1, 0:16], 0.0),
                 reads=(), writes=("bs0", "out0"), tag="bz")
            dma("s", "dsc", baser[0:1, 16:512],
                cnt_d[:, :].rearrange("a b -> (a b)")[None, 0:496], ("cntd",),
                ("bs0", "out0"), "cnt_ld")
            cur, nxt = 0, 1024
            curk, nxtk = "bs0", "bs1"
            curo, nxto = "out0", "out1"
            for r in range(5):
                sh = 16 * (2 ** r)
                B.op("v", (lambda c_, n_, s_: (lambda eng: eng.tensor_copy(
                         baser[0:1, n_:n_ + s_], baser[0:1, c_:c_ + s_])))(cur, nxt, sh),
                     reads=(curk,), writes=(nxtk, nxto), tag=f"bc{r}")
                B.op("v", (lambda c_, n_, s_: (lambda eng: eng.tensor_add(
                         baser[0:1, n_ + s_:n_ + 512], baser[0:1, c_ + s_:c_ + 512],
                         baser[0:1, c_:c_ + 512 - s_])))(cur, nxt, sh),
                     reads=(curk,), writes=(nxtk, nxto), tag=f"ba{r}")
                cur, nxt = nxt, cur
                curk, nxtk = nxtk, curk
                curo, nxto = nxto, curo
            B.op("v", (lambda c_: (lambda eng: eng.tensor_copy(
                     baseb[0:1, 0:512], baser[0:1, c_:c_ + 512])))(cur),
                 reads=(curk,), writes=("bb", "silb0"), tag="b_a")
            B.op("v", (lambda c_, n_: (lambda eng: eng.tensor_sub(
                     baser[0:1, n_:n_ + 512], baser[0:1, c_:c_ + 512],
                     baseb[0:1, 0:512])))(cur, nxt),
                 reads=(curk, "bb"), writes=(nxtk, nxto), tag="b_d")
            B.op("v", (lambda n_: (lambda eng: eng.tensor_copy(
                     baseb[0:1, 512:1024], baser[0:1, n_:n_ + 512])))(nxt),
                 reads=(nxtk, "bb"), writes=("bb", "silb0"), tag="b_b")

        def emit_A2(si):
            base = slot_base(si)
            sk = slot_key(si)
            for i in range(NB):
                p = i % 2
                so = 72 * p
                pk = "postm"
                pslice = postm[:, p * 16:(p + 1) * 16]
                for kvi, w0 in enumerate([0, 8192]):
                    for dh in range(2):
                        bank = kvi * 2 + dh
                        for c in range(8):
                            B.op("p", (lambda bk_, cc, ww, dd, ii: (lambda eng: eng.matmul(
                                     ps[bk_][:, :],
                                     xnf[:, cc * TOK + ii * 128: cc * TOK + ii * 128 + 128],
                                     wc(base + ww + cc * 1024 + dd * 512, 512),
                                     start=(cc == 0), stop=False)))(bank, c, w0, dh, i),
                                 reads=(f"xnf{i}", sk), writes=(f"ps{bank}",),
                                 sem=None, tag=f"kv{i}_{kvi}_{dh}_{c}")
                        B.op("p", (lambda bk_, kk_, dd: (lambda eng: eng.matmul(
                                 ps[bk_][:, :], ones_s[0:1, 0:128],
                                 crow[0:1, kk_ * 1024 + dd * 512: kk_ * 1024 + dd * 512 + 512],
                                 start=False, stop=True)))(bank, kvi, dh),
                             reads=("ones", "crow"), writes=(f"ps{bank}",), sem="sp",
                             tag=f"kvb{i}_{kvi}_{dh}")
                        B.op("a", (lambda bk_, kk_, dd: (lambda eng: eng.activation(
                                 kv[:, kk_ * 1024 + dd * 512: kk_ * 1024 + dd * 512 + 512],
                                 ps[bk_][:, :], AF.Copy)))(bank, kvi, dh),
                             reads=(f"ps{bank}",), writes=(f"kvp{kvi}",),
                             tag=f"kve{i}_{kvi}_{dh}")
                for h in range(NH):
                    p0 = (h % 2) * 64
                    off = (h // 2) * 64
                    B.op("p", (lambda pp, oo, hh_, ii: (lambda eng: eng.matmul(
                             ps[4][pp:pp + 64, oo:oo + 64],
                             kv[:, 1024 + hh_ * 64: 1024 + (hh_ + 1) * 64],
                             kv[:, hh_ * 64: (hh_ + 1) * 64],
                             start=False, stop=(ii == NB - 1),
                             skip_group_check=True)))(p0, off, h, i),
                         reads=("kvp0", "kvp1"), writes=("ps4",),
                         sem=("sp" if h == NH - 1 else None), tag=f"sp{i}_{h}")

                B.op("p", (lambda ii: (lambda eng: eng.matmul(
                         ps[5][:, 0:16], lt_s[:, :],
                         mkall[:, ii * 16:(ii + 1) * 16],
                         start=True, stop=False)))(i),
                     reads=("lt", "mka"), writes=("ps5l",), sem=None,
                     tag=f"pos{i}")
                B.op("p", (lambda ii: (lambda eng: eng.matmul(
                         ps[5][:, 0:16], ones_s[0:1, 0:128],
                         silb[0:1, ii * 16:(ii + 1) * 16],
                         start=False, stop=False)))(i),
                     reads=("ones", "bb"), writes=("ps5l",), sem=None,
                     tag=f"poshi{i}")
                B.op("p", (lambda ii: (lambda eng: eng.matmul(
                         ps[5][:, 0:16], ones_s[0:1, 0:128],
                         silb[0:1, 512 + ii * 16: 512 + (ii + 1) * 16],
                         start=False, stop=True)))(i),
                     reads=("ones", "bb"), writes=("ps5l",), sem="sp",
                     tag=f"poslo{i}")
                B.op("a", (lambda ps_: (lambda eng: eng.activation(
                         ps_, ps[5][:, 0:16], AF.Copy)))(pslice),
                     reads=("ps5l",), writes=(pk,), tag=f"pose{i}")
                B.op("v", (lambda o, ii: (lambda eng: eng.max(
                         out=stats[:, o + 8:o + 16],
                         in_=lgall[:, ii * 16:(ii + 1) * 16])))(so, i),
                     reads=("lga",), writes=(f"st_m8{p}",), tag=f"m_max{i}")
                B.op("v", (lambda o: (lambda eng: eng.tensor_scalar(
                         stats[:, o + 64:o + 65], stats[:, o + 8:o + 9],
                         -1.0, None, op0=ALU.mult)))(so),
                     reads=(f"st_m8{p}",), writes=(f"st_nm1{p}",), tag=f"g_nm{i}")
                B.op("a", (lambda o, ii: (lambda eng: eng.activation(
                         stats[:, o + 48:o + 64], lgall[:, ii * 16:(ii + 1) * 16],
                         AF.Exp, bias=stats[:, o + 64:o + 65])))(so, i),
                     reads=("lga", f"st_nm1{p}"), writes=(f"st_ex{p}",),
                     tag=f"g_exp{i}")
                B.op("v", (lambda o, ii: (lambda eng: eng.tensor_mul(
                         stats[:, o + 48:o + 64], stats[:, o + 48:o + 64],
                         mkall[:, ii * 16:(ii + 1) * 16])))(so, i),
                     reads=(f"st_ex{p}", "mka"), writes=(f"st_ex{p}",),
                     tag=f"g_em{i}")
                B.op("v", (lambda o: (lambda eng: eng.reduce_sum(
                         stats[:, o + 65:o + 66], stats[:, o + 48:o + 64], axis=AX)))(so),
                     reads=(f"st_ex{p}",), writes=(f"st_ssum{p}",), tag=f"g_sum{i}")
                B.op("v", (lambda o: (lambda eng: eng.reciprocal(
                         stats[:, o + 66:o + 67], stats[:, o + 65:o + 66])))(so),
                     reads=(f"st_ssum{p}",), writes=(f"st_rec{p}",), tag=f"g_rec{i}")
                B.op("v", (lambda o, ii: (lambda eng: eng.tensor_scalar(
                         gall[:, ii * 16:(ii + 1) * 16], stats[:, o + 48:o + 64],
                         stats[:, o + 66:o + 67], None, op0=ALU.mult)))(so, i),
                     reads=(f"st_ex{p}", f"st_rec{p}"), writes=("gall",),
                     tag=f"g_out{i}")

                for c in range(8):
                    tb_ = 7 if c % 2 == 0 else 6
                    bview = (ps7b if tb_ == 7 else ps6b)[:, 0:128]
                    B.op("p", (lambda bv, cc, ii: (lambda eng: eng.transpose(
                             bv, xnf[:, cc * TOK + ii * 128: cc * TOK + ii * 128 + 128],
                             ident_s[:, :])))(bview, c, i),
                         reads=(f"xnf{i}", "ident"), writes=(f"ps{tb_}",),
                         sem="sp", tag=f"mtp{i}_{c}")
                    if c % 2 == 0:
                        B.op("a", (lambda bv, cc, pp: (lambda eng: eng.activation(
                                 kv2[:, pp * 1024 + cc * 128: pp * 1024 + (cc + 1) * 128],
                                 bv, AF.Copy)))(bview, c, p),
                             reads=(f"ps{tb_}",), writes=(f"kvp{2 + p}",), tag=f"mtpe{i}_{c}")
                    else:
                        B.op("v", (lambda bv, cc, pp: (lambda eng: eng.tensor_copy(
                                 kv2[:, pp * 1024 + cc * 128: pp * 1024 + (cc + 1) * 128],
                                 bv)))(bview, c, p),
                             reads=(f"ps{tb_}",), writes=(f"kvp{2 + p}",), tag=f"mtpe{i}_{c}")

                for s in range(3):
                    B.op("v", (lambda o, ii, ss: (lambda eng: eng.tensor_scalar(
                             stats[:, o + 32:o + 48], lgall[:, ii * 16:(ii + 1) * 16],
                             stats[:, o + 8 + ss:o + 9 + ss], None,
                             op0=ALU.is_equal)))(so, i, s),
                         reads=("lga", f"st_m8{p}"), writes=(f"sel{p}",),
                         tag=f"m_sel{i}_{s}")
                    B.op("v", (lambda o: (lambda eng: eng.scalar_tensor_tensor(
                             stats[:, o + 16:o + 32], stats[:, o + 32:o + 48], 1.0,
                             ecol_s[:, :], op0=ALU.mult, op1=ALU.mult,
                             accum_out=stats[:, o + 67:o + 68])))(so),
                         reads=(f"sel{p}", "ec"), writes=(f"dmp{p}", f"ec{p}"),
                         tag=f"m_e{i}_{s}")
                    B.op("v", (lambda o, ps_: (lambda eng: eng.scalar_tensor_tensor(
                             stats[:, o + 16:o + 32], stats[:, o + 32:o + 48], 1.0,
                             ps_, op0=ALU.mult, op1=ALU.mult,
                             accum_out=stats[:, o + 68:o + 69])))(so, pslice),
                         reads=(f"sel{p}", pk), writes=(f"dmp{p}", f"pc{p}"),
                         tag=f"m_p{i}_{s}")
                    B.op("v", (lambda o, ii, ss: (lambda eng: eng.scalar_tensor_tensor(
                             stats[:, o + 16:o + 32], stats[:, o + 32:o + 48], 1.0,
                             gall[:, ii * 16:(ii + 1) * 16], op0=ALU.mult,
                             op1=ALU.mult,
                             accum_out=gsl[:, ii * 3 + ss:ii * 3 + ss + 1])))(so, i, s),
                         reads=(f"sel{p}", "gall"), writes=(f"dmp{p}", "gsl"),
                         tag=f"m_g{i}_{s}")
                    B.op("v", (lambda o: (lambda eng: eng.tensor_scalar(
                             stats[:, o + 7:o + 8], stats[:, o + 67:o + 68],
                             float(CS), stats[:, o + 68:o + 69],
                             op0=ALU.mult, op1=ALU.add)))(so),
                         reads=(f"ec{p}", f"pc{p}"), writes=(f"df{p}",),
                         tag=f"m_d{i}_{s}")
                    B.op("a", (lambda o, ii, ss: (lambda eng: eng.activation(
                             dsti[:, ii * 3 + ss:ii * 3 + ss + 1],
                             stats[:, o + 7:o + 8], AF.Copy)))(so, i, s),
                         reads=(f"df{p}",), writes=("dsti",), tag=f"m_ci{i}_{s}")
                    B.op("g", (lambda ii, ss, pp: (lambda eng: eng.indirect_dma_start(
                             xe_d[:, :],
                             IndirectOffsetOnAxis(dsti[:, ii * 3 + ss:ii * 3 + ss + 1], 0),
                             kv2[:, pp * 1024:(pp + 1) * 1024], None)))(i, s, p),
                         reads=("dsti", f"kvp{2 + p}"), writes=("xed",), sem="dsc",
                         amt=16, tag=f"m_sc{i}_{s}")

            B.op("v", lambda eng: eng.tensor_scalar(
                     spsb[:, :], ps[4][:, :], SCALE, None, op0=ALU.mult),
                 reads=("ps4",), writes=("spsb",), tag="spe")

        def emit_B(si):
            _, ex, hh = steps[si]
            base = slot_base(si)
            sk = slot_key(si)
            ep = ex % 2

            if hh == 0:
                for blk in range(8):
                    bp = blk % 4
                    for c in range(8):
                        tb_ = 7 if c % 2 == 0 else 6
                        bview = (ps7b if tb_ == 7 else ps6b)[:, 0:128]
                        B.op("p", (lambda bv, cc, bb_, bpp: (lambda eng: eng.transpose(
                                 bv, kvt(bpp)[0][:, kvt(bpp)[1] + cc * 128: kvt(bpp)[1] + (cc + 1) * 128],
                                 ident_s[:, :])))(bview, c, blk, bp),
                             reads=(f"kvp{bp}", "ident"), writes=(f"ps{tb_}",),
                             sem="sp", tag=f"xt_{ex}_{blk}_{c}")
                        if c % 2 == 0:
                            B.op("a", (lambda bv, cc, bb_: (lambda eng: eng.activation(
                                     xef[:, cc * 1024 + bb_ * 128: cc * 1024 + bb_ * 128 + 128],
                                     bv, AF.Copy)))(bview, c, blk),
                                 reads=(f"ps{tb_}",), writes=("xef",), tag=f"xte_{ex}_{blk}_{c}")
                        else:
                            B.op("v", (lambda bv, cc, bb_: (lambda eng: eng.tensor_copy(
                                     xef[:, cc * 1024 + bb_ * 128: cc * 1024 + bb_ * 128 + 128],
                                     bv)))(bview, c, blk),
                                 reads=(f"ps{tb_}",), writes=("xef",), tag=f"xte_{ex}_{blk}_{c}")
                    if blk + 4 < 8:
                        emit_xe_blk(ex, blk + 4)

            for tt in range(2):
                for m in range(8):
                    bank = m % 4
                    for c in range(8):
                        B.op("p", (lambda bk_, cc, mm, t_: (lambda eng: eng.matmul(
                                 ps[bk_][:, :],
                                 wc(base + cc * 1024 + mm * 128, 128),
                                 xef[:, cc * 1024 + t_ * 512: cc * 1024 + (t_ + 1) * 512],
                                 start=(cc == 0), stop=(cc == 7))))(bank, c, m, tt),
                             reads=(sk, "xef"), writes=(f"ps{bank}",),
                             sem=("sp" if c == 7 else None),
                             tag=f"w1_{ex}_{hh}_{tt}_{m}_{c}")
                    B.op("a", (lambda bk_, mm, e_: (lambda eng: eng.activation(
                             hidq[:, mm * 512:(mm + 1) * 512], ps[bk_][:, :],
                             AF.Relu, bias=b1col[:, e_ * 16 + hh * 8 + mm:
                                                 e_ * 16 + hh * 8 + mm + 1])))(bank, m, ex),
                         reads=(f"ps{bank}", "b1c"), writes=(f"hid{m}",),
                         tag=f"w1e_{ex}_{hh}_{tt}_{m}")
                for tb in range(4):
                    blk = tt * 4 + tb
                    for dh in range(2):
                        bank = 4 + (tb * 2 + dh + 2) % 4
                        for j in range(8):
                            last = (j == 7 and hh == 1)
                            B.op("p", (lambda bk_, jj, t_, dd, ll: (lambda eng: eng.matmul(
                                     ps[bk_][:, :],
                                     hidq[:, jj * 512 + t_ * 128: jj * 512 + (t_ + 1) * 128],
                                     wc(base + 8192 + jj * 1024 + dd * 512, 512),
                                     start=(jj == 0), stop=ll)))(bank, j, tb, dh, last),
                                 reads=(f"hid{j}", sk), writes=(f"ps{bank}",),
                                 sem=("sp" if last else None),
                                 tag=f"w2_{ex}_{hh}_{tt}_{tb}_{dh}_{j}")
                        bp = blk % 2
                        if hh == 0:
                            B.op("p", (lambda bk_, dd: (lambda eng: eng.matmul(
                                     ps[bk_][:, :], ones_s[0:1, 0:128],
                                     b2row[0:1, dd * 512:(dd + 1) * 512],
                                     start=False, stop=True)))(bank, dh),
                                 reads=("ones", "b2row"), writes=(f"ps{bank}",),
                                 sem="sp", tag=f"w2b_{ex}_{hh}_{tt}_{tb}_{dh}")
                        B.op("v", (lambda bk_, bpp, dd: (lambda eng: eng.tensor_copy(
                                 outsb[:, bpp * 1024 + dd * 512: bpp * 1024 + dd * 512 + 512],
                                 ps[bk_][:, :])))(bank, bp, dh),
                             reads=(f"ps{bank}",), writes=(f"ob{bp}",),
                             tag=f"obe_{ex}_{hh}_{tt}_{tb}_{dh}")
                    bp = blk % 2
                    if hh == 0:
                        B.op("s", (lambda bb_, e_, bpp: (lambda eng: eng.dma_start(
                                 oe_d[e_ * CS + bb_ * 128: e_ * CS + (bb_ + 1) * 128, :],
                                 outsb[:, bpp * 1024:(bpp + 1) * 1024])))(blk, ex, bp),
                             reads=(f"ob{bp}",), writes=(f"od{ex}_{blk}",), sem="dh0",
                             amt=16, tag=f"ost_{ex}_{hh}_{blk}")
                    else:
                        B.op("g", (lambda bb_, e_, bpp: (lambda eng: eng.dma_start(
                                 oe_d[e_ * CS + bb_ * 128: e_ * CS + (bb_ + 1) * 128, :],
                                 outsb[:, bpp * 1024:(bpp + 1) * 1024],
                                 accum_op=ALU.add)))(blk, ex, bp),
                             reads=(f"ob{bp}",), writes=(f"od{ex}_{blk}",), sem="dh1",
                             amt=16, tag=f"ost_{ex}_{hh}_{blk}")

        def emit_C(si):
            base = slot_base(si)
            sk = slot_key(si)

            for c in range(8):
                for dh in range(2):
                    bank = (c * 2 + dh) % 4
                    for sub in range(2):
                        B.op("p", (lambda bk_, cc, dd, ss: (lambda eng: eng.matmul(
                                 ps[bk_][ss * 64:(ss + 1) * 64, :],
                                 spsb[ss * 64:(ss + 1) * 64, cc * 64:(cc + 1) * 64],
                                 wc(base + 8192 + cc * 1024 + dd * 512, 512)[ss * 64:(ss + 1) * 64, :],
                                 start=True, stop=True)))(bank, c, dh, sub),
                             reads=("spsb", sk), writes=(f"ps{bank}",),
                             sem=("sp" if sub == 1 else None),
                             tag=f"T_{c}_{dh}_{sub}")
                    B.op("a", (lambda bk_, cc, dd: (lambda eng: eng.activation(
                             wc(base + 8192 + cc * 1024 + dd * 512, 512),
                             ps[bk_][:, :], AF.Copy)))(bank, c, dh),
                         reads=(f"ps{bank}",), writes=(sk,), tag=f"Te_{c}_{dh}")

            for gt in range(NTT):
                xnfk = tuple(f"xnf{bb}" for bb in range(gt * 4, gt * 4 + 4))
                for co in range(8):
                    bank = co % 4
                    for c in range(8):
                        B.op("p", (lambda bk_, cc, oo, g_: (lambda eng: eng.matmul(
                                 ps[bk_][:, :],
                                 wc(base + cc * 1024 + oo * 128, 128),
                                 xnf[:, cc * TOK + g_ * 512: cc * TOK + (g_ + 1) * 512],
                                 start=(cc == 0), stop=False)))(bank, c, co, gt),
                             reads=(sk,) + xnfk, writes=(f"ps{bank}",), sem=None,
                             tag=f"q_{gt}_{co}_{c}")
                    B.op("p", (lambda bk_, oo: (lambda eng: eng.matmul(
                             ps[bk_][:, :], crow[0:1, oo * 128:(oo + 1) * 128],
                             ones_s[0:1, 0:512], start=False, stop=True)))(bank, co),
                         reads=("crow", "ones"), writes=(f"ps{bank}",), sem="sp",
                         tag=f"qb_{gt}_{co}")
                    B.op("a", (lambda bk_, oo: (lambda eng: eng.activation(
                             hidq[:, oo * 512:(oo + 1) * 512], ps[bk_][:, :],
                             AF.Copy)))(bank, co),
                         reads=(f"ps{bank}",), writes=(f"hid{co}",),
                         tag=f"qe_{gt}_{co}")

                for tb in range(4):
                    bi = gt * 4 + tb
                    p = bi % 2
                    xk = f"xin{p}"
                    if tb == 0 and gt == 0:
                        for j in (0, 1):
                            dma("s", f"dx{j % 2}",
                                xin[:, (j % 2) * 1024:(j % 2) * 1024 + 1024],
                                x_d[j * 128:(j + 1) * 128, :], (), (f"xin{j % 2}",),
                                f"ld_xc{j}")
                    for s in range(3):
                        B.op("g", (lambda ii, ss, pp: (lambda eng: eng.indirect_dma_start(
                                 xef[:, pp * 4096 + ss * 1024: pp * 4096 + (ss + 1) * 1024],
                                 None, oe_d[:, :],
                                 IndirectOffsetOnAxis(
                                     dsti[:, ii * 3 + ss:ii * 3 + ss + 1], 0))))(bi, s, p),
                             reads=("dsti", "odz", f"gth{p}")
                                   + tuple(f"od{e_}_{b_}" for e_ in range(NE)
                                           for b_ in range(8)),
                             writes=(f"gth{p}",),
                             sem=("dg0" if p == 0 else "dg1"), amt=16,
                             tag=f"gth_{bi}_{s}")
                    B.op("v", (lambda ii, pp: (lambda eng: eng.tensor_scalar(
                             acct[:, pp * 1024:(pp + 1) * 1024],
                             xef[:, pp * 4096: pp * 4096 + 1024],
                             gsl[:, ii * 3:ii * 3 + 1], None, op0=ALU.mult)))(bi, p),
                         reads=(f"gth{p}", "gsl"), writes=(f"acct{p}",),
                         tag=f"cmb0_{bi}")
                    for s in (1, 2):
                        B.op("v", (lambda ii, ss, pp: (lambda eng:
                                 eng.scalar_tensor_tensor(
                                     acct[:, pp * 1024:(pp + 1) * 1024],
                                     xef[:, pp * 4096 + ss * 1024: pp * 4096 + (ss + 1) * 1024],
                                     gsl[:, ii * 3 + ss:ii * 3 + ss + 1],
                                     acct[:, pp * 1024:(pp + 1) * 1024],
                                     op0=ALU.mult, op1=ALU.add)))(bi, s, p),
                             reads=(f"gth{p}", "gsl", f"acct{p}"), writes=(f"acct{p}",),
                             tag=f"cmb{s}_{bi}")
                    B.op("a", (lambda pp: (lambda eng: eng.activation(
                             silb[:, pp * 1024:(pp + 1) * 1024],
                             acct[:, pp * 1024:(pp + 1) * 1024], AF.Silu)))(p),
                         reads=(f"acct{p}",), writes=(f"silb{p}",), tag=f"sil_{bi}")
                    for dh in range(2):
                        bank = 4 + (tb * 2 + dh) % 4
                        for c in range(8):
                            B.op("p", (lambda bk_, cc, t_, dd: (lambda eng: eng.matmul(
                                     ps[bk_][:, :],
                                     hidq[:, cc * 512 + t_ * 128: cc * 512 + (t_ + 1) * 128],
                                     wc(base + 8192 + cc * 1024 + dd * 512, 512),
                                     start=(cc == 0), stop=False)))(bank, c, tb, dh),
                                 reads=(f"hid{c}", sk), writes=(f"ps{bank}",),
                                 sem=None, tag=f"at_{bi}_{dh}_{c}")
                        B.op("p", (lambda bk_, dd: (lambda eng: eng.matmul(
                                 ps[bk_][:, :], ones_s[0:1, 0:128],
                                 crow[0:1, 1024 + dd * 512: 1024 + (dd + 1) * 512],
                                 start=False, stop=True)))(bank, dh),
                             reads=("ones", "crow"), writes=(f"ps{bank}",), sem="sp",
                             tag=f"atb_{bi}_{dh}")
                        B.op("v", (lambda bk_, pp, dd: (lambda eng: eng.tensor_mul(
                                 outt[:, pp * 1024 + dd * 512: pp * 1024 + dd * 512 + 512],
                                 ps[bk_][:, :],
                                 silb[:, pp * 1024 + dd * 512: pp * 1024 + dd * 512 + 512])))(bank, p, dh),
                             reads=(f"ps{bank}", f"silb{p}"), writes=(f"out{p}",),
                             tag=f"om_{bi}_{dh}")
                    B.op("v", (lambda pp: (lambda eng: eng.tensor_add(
                             outt[:, pp * 1024:(pp + 1) * 1024],
                             outt[:, pp * 1024:(pp + 1) * 1024],
                             xin[:, pp * 1024:(pp + 1) * 1024])))(p),
                         reads=(f"out{p}", xk), writes=(f"out{p}",),
                         tag=f"oa_{bi}")
                    dma("s", f"dy{p}", y_d[bi * 128:(bi + 1) * 128, :],
                        outt[:, p * 1024:(p + 1) * 1024], (f"out{p}",), (),
                        f"st_y_{bi}")
                    j = bi + 2
                    if j < NB:
                        dma("s", f"dx{j % 2}",
                            xin[:, (j % 2) * 1024:(j % 2) * 1024 + 1024],
                            x_d[j * 128:(j + 1) * 128, :], (), (f"xin{j % 2}",),
                            f"ld_xc{j}")

        for si, st in enumerate(steps):
            if st[0] == "A":
                emit_A1(si)
                emit_base()
                emit_A2(si)
            elif st[0] == "B":
                emit_B(si)
            else:
                emit_C(si)
            if si + 1 < len(steps) and steps[si + 1][0] == "B" and steps[si + 1][2] == 0:
                emit_xe(steps[si + 1][1])
            if si + 2 < len(steps):
                emit_loads(si + 2)

        tot0, tot1 = B.cnt.get("dy0", 0), B.cnt.get("dy1", 0)
        B.finish()

        items = B.items

        def run_engine(eng_obj, ename):
            for (en, wlist, fn, sem, amt, tag) in items:
                if en != ename:
                    continue
                for (s, v) in wlist:
                    eng_obj.wait_ge(sems[s], v)
                inst = fn(eng_obj)
                if sem is not None:
                    inst.then_inc(sems[sem], amt)
            if ename == "s":
                eng_obj.wait_ge(sems["dy0"], tot0)
                eng_obj.wait_ge(sems["dy1"], tot1)

        with nc.Block() as block:
            @block.tensor
            def _(eng):
                run_engine(eng, "p")

            @block.vector
            def _(eng):
                run_engine(eng, "v")

            @block.scalar
            def _(eng):
                run_engine(eng, "a")

            @block.gpsimd
            def _(eng):
                run_engine(eng, "g")

            @block.sync
            def _(eng):
                run_engine(eng, "s")

    return nc


_NC_CACHE = None


def kernel(L0_attn_x, L0_attn_Wq, L0_attn_bq, L0_attn_Wk, L0_attn_bk,
           L0_attn_Wv, L0_attn_bv, L0_attn_Wo, L0_attn_bo,
           L0_ffn_gate, L0_ffn_W1, L0_ffn_b1, L0_ffn_W2, L0_ffn_b2):
    global _NC_CACHE
    if _NC_CACHE is None:
        _NC_CACHE = build_nc()
    nc = _NC_CACHE

    x = np.ascontiguousarray(np.asarray(L0_attn_x, dtype=np.float32))
    shared = {
        "wq": np.ascontiguousarray(L0_attn_Wq, dtype=np.float32),
        "wk": np.ascontiguousarray(L0_attn_Wk, dtype=np.float32),
        "wv": np.ascontiguousarray(L0_attn_Wv, dtype=np.float32),
        "wo": np.ascontiguousarray(L0_attn_Wo, dtype=np.float32),
        "bq": np.ascontiguousarray(L0_attn_bq, dtype=np.float32),
        "bk": np.ascontiguousarray(L0_attn_bk, dtype=np.float32),
        "bv": np.ascontiguousarray(L0_attn_bv, dtype=np.float32),
        "bo": np.ascontiguousarray(L0_attn_bo, dtype=np.float32),
        "gate": np.ascontiguousarray(L0_ffn_gate, dtype=np.float32),
        "w1": np.ascontiguousarray(L0_ffn_W1, dtype=np.float32),
        "b1": np.ascontiguousarray(L0_ffn_b1, dtype=np.float32),
        "w2": np.ascontiguousarray(L0_ffn_W2, dtype=np.float32),
        "b2": np.ascontiguousarray(L0_ffn_b2, dtype=np.float32),
    }
    in_maps = [dict(shared, x=np.ascontiguousarray(x[c])) for c in range(8)]
    res = run_bass_kernel_spmd(nc, in_maps, core_ids=list(range(8)))
    return np.stack([np.asarray(r["y"], dtype=np.float32) for r in res.results])


# version 28
# speedup vs baseline: 1.0016x; 1.0016x over previous
import numpy as np
import ml_dtypes
from contextlib import ExitStack

import concourse.bass as bass
import concourse.mybir as mybir
from concourse.bass import IndirectOffsetOnAxis
from concourse.bass_utils import run_bass_kernel_spmd

F32 = mybir.dt.float32
BF16 = mybir.dt.bfloat16
I32 = mybir.dt.int32
AF = mybir.ActivationFunctionType
ALU = mybir.AluOpType
AX = mybir.AxisListType.X

TOK = 4096
DM = 1024
HID = 2048
NE = 16
NH, HD = 16, 64
NB = TOK // 128
NTT = TOK // 512
SCALE = 0.125
CAP = 1024
CS = 1280
NROW = NE * CS


class Builder:
    ENG2SEM = {"p": "sp", "a": "sa", "v": "sv", "g": "sg", "s": "ss"}

    def __init__(self):
        self.items = []
        self.cnt = {}
        self.wm = {}
        self.reg = {}
        self.pend = {}
        self.sem_names = set(self.ENG2SEM.values())
        self.dma_sems = set()

    def op(self, eng, fn, reads=(), writes=(), sem="ENG", amt=1, tag=""):
        waits = {}

        def need(m):
            if m is None:
                return
            s, v, meng = m
            if meng == "p" and eng == "p":
                return
            assert v is not None, f"unresolved marker ({s}) needed by {tag}"
            if s in self.dma_sems:
                v = self.cnt[s]
            if waits.get(s, 0) < v:
                waits[s] = v

        for k in reads:
            st = self.reg.get(k)
            if st:
                need(st["w"])
        for k in writes:
            st = self.reg.get(k)
            if st:
                need(st["w"])
                for m in st["r"]:
                    need(m)
        wmm = self.wm.setdefault(eng, {})
        wlist = []
        for s, v in sorted(waits.items()):
            if wmm.get(s, 0) < v:
                wlist.append((s, v))
                wmm[s] = v
        if sem == "ENG":
            sem = self.ENG2SEM[eng]
        if sem is not None:
            self.sem_names.add(sem)
            if amt != 1:
                self.dma_sems.add(sem)
            self.cnt[sem] = self.cnt.get(sem, 0) + amt
            newv = self.cnt[sem]
            for m in self.pend.pop(sem, []):
                m[1] = newv
            marker = [sem, newv, eng]
        else:
            esem = self.ENG2SEM[eng]
            marker = [esem, None, eng]
            self.pend.setdefault(esem, []).append(marker)
        self.items.append((eng, wlist, fn, sem, amt, tag))
        for k in writes:
            self.reg[k] = {"w": marker, "r": []}
        for k in reads:
            self.reg.setdefault(k, {"w": None, "r": []})["r"].append(marker)

    def finish(self):
        leftover = {k: v for k, v in self.pend.items() if v}
        assert not leftover, f"pending markers never resolved: {list(leftover)}"


def build_nc():
    nc = bass.Bass("TRN2", target_bir_lowering=False)

    x_d = nc.dram_tensor("x", [TOK, DM], F32, kind="ExternalInput")
    wq_d = nc.dram_tensor("wq", [DM, DM], F32, kind="ExternalInput")
    wk_d = nc.dram_tensor("wk", [DM, DM], F32, kind="ExternalInput")
    wv_d = nc.dram_tensor("wv", [DM, DM], F32, kind="ExternalInput")
    wo_d = nc.dram_tensor("wo", [DM, DM], F32, kind="ExternalInput")
    bq_d = nc.dram_tensor("bq", [DM], F32, kind="ExternalInput")
    bk_d = nc.dram_tensor("bk", [DM], F32, kind="ExternalInput")
    bv_d = nc.dram_tensor("bv", [DM], F32, kind="ExternalInput")
    bo_d = nc.dram_tensor("bo", [DM], F32, kind="ExternalInput")
    gate_d = nc.dram_tensor("gate", [DM, NE], F32, kind="ExternalInput")
    w1_d = nc.dram_tensor("w1", [NE, DM, HID], F32, kind="ExternalInput")
    b1_d = nc.dram_tensor("b1", [NE, HID], F32, kind="ExternalInput")
    w2_d = nc.dram_tensor("w2", [NE, HID, DM], F32, kind="ExternalInput")
    b2_d = nc.dram_tensor("b2", [NE, DM], F32, kind="ExternalInput")
    y_d = nc.dram_tensor("y", [TOK, DM], F32, kind="ExternalOutput")

    xe_d = nc.dram_tensor("xed", [NROW, DM], BF16, kind="Internal")
    oe_d = nc.dram_tensor("oed", [NROW, DM], BF16, kind="Internal")
    cnt_d = nc.dram_tensor("cntd", [NB, NE], F32, kind="Internal")

    ident_d = nc.inline_tensor(np.eye(128, dtype=ml_dtypes.bfloat16), "identc")
    ones_d = nc.inline_tensor(np.ones((1, 512), dtype=ml_dtypes.bfloat16), "onesc")
    lt_np = np.triu(np.ones((128, 128), dtype=np.float32), 1).astype(
        ml_dtypes.bfloat16)
    lt_d = nc.inline_tensor(np.ascontiguousarray(lt_np), "ltc")
    onescol_d = nc.inline_tensor(np.ones((128, 1), dtype=ml_dtypes.bfloat16), "occ")
    identf_d = nc.inline_tensor(np.eye(16, dtype=np.float32), "idfc")
    ecols_d = nc.inline_tensor(
        np.broadcast_to(np.arange(16, dtype=np.float32), (128, 16)).copy(), "ecc")

    B = Builder()

    with ExitStack() as ctx:
        e = ctx.enter_context
        xnf = e(nc.sbuf_tensor("xnf", [128, 8 * TOK], BF16))
        wbuf = e(nc.sbuf_tensor("wbuf", [128, 32768], BF16))
        xef = e(nc.sbuf_tensor("xef", [128, 8192], BF16))
        hidq = e(nc.sbuf_tensor("hidq", [128, 8 * 512], BF16))
        kv = e(nc.sbuf_tensor("kv", [128, 2048], BF16))
        kv2 = e(nc.sbuf_tensor("kv2", [128, 2048], BF16))
        kvt = lambda pp: (kv if pp < 2 else kv2, (pp % 2) * 1024)
        xin = e(nc.sbuf_tensor("xin", [128, 2048], F32))
        outt = e(nc.sbuf_tensor("outt", [128, 2048], F32))
        silb = e(nc.sbuf_tensor("silb", [128, 2048], BF16))
        outsb = silb
        acct = e(nc.sbuf_tensor("acct", [128, 2048], F32))
        spsb = e(nc.sbuf_tensor("spsb", [128, 512], BF16))
        gall = e(nc.sbuf_tensor("gall", [128, NB * NE], F32))
        lgall = e(nc.sbuf_tensor("lgall", [128, NB * NE], F32))
        mkall = e(nc.sbuf_tensor("mkall", [128, NB * NE], BF16))
        dsti = e(nc.sbuf_tensor("dsti", [128, NB * 3], I32))
        gsl = e(nc.sbuf_tensor("gsl", [128, NB * 3], F32))
        postm = e(nc.sbuf_tensor("postm", [128, 64], F32))
        stats = e(nc.sbuf_tensor("stats", [128, 148], F32))
        crow = e(nc.sbuf_tensor("crow", [1, 2048], BF16))
        b2row = e(nc.sbuf_tensor("b2row", [1, DM], BF16))
        b1col = e(nc.sbuf_tensor("b1col", [128, 256], F32))
        b1stg = e(nc.sbuf_tensor("b1stg", [16, 256], F32))
        identf = e(nc.sbuf_tensor("identf", [16, 16], F32))
        gate_s = e(nc.sbuf_tensor("gate_s", [128, 128], BF16))
        ident_s = e(nc.sbuf_tensor("ident_s", [128, 128], BF16))
        lt_s = e(nc.sbuf_tensor("lt_s", [128, 128], BF16))
        ecol_s = e(nc.sbuf_tensor("ecol_s", [128, 16], F32))
        onescol_s = e(nc.sbuf_tensor("onescol_s", [128, 1], BF16))
        ones_s = e(nc.sbuf_tensor("ones_s", [1, 512], BF16))

        ps = [e(nc.psum_tensor(f"ps{i}", [128, 512], F32)) for i in range(8)]
        ps6b = ps[6][:, :].bitcast(BF16)
        ps7b = ps[7][:, :].bitcast(BF16)

        sems = {}
        for name in ["sp", "sa", "sv", "sg", "ss",
                     "dwA", "dwB", "dxe", "dbc0", "dbc1", "db2", "dcst",
                     "dcr", "dsc", "dg0", "dg1", "dh0", "dh1", "dz",
                     "dx0", "dx1", "dy0", "dy1"]:
            sems[name] = e(nc.semaphore(name))

        def dma(queue_eng, sem, out_ap, in_ap, reads, writes, tag):
            B.op(queue_eng,
                 (lambda o, i: (lambda eng: eng.dma_start(o, i)))(out_ap, in_ap),
                 reads=reads, writes=writes, sem=sem, amt=16, tag=tag)

        def wc(col0, ncols):
            return wbuf[:, col0:col0 + ncols]

        def emit_xe_blk(ex, blk):
            bp = blk % 4
            tt_, c0 = kvt(bp)
            dma("s", "dxe", tt_[:, c0:c0 + 1024],
                xe_d[ex * CS + blk * 128: ex * CS + (blk + 1) * 128, :],
                ("xed",), (f"kvp{bp}",), f"xe_{ex}_{blk}")

        def emit_xe(ex):
            for blk in range(4):
                emit_xe_blk(ex, blk)


        steps = [("A",)]
        for ex in range(NE):
            for hh in range(2):
                steps.append(("B", ex, hh))
        steps.append(("C",))

        def slot_base(si):
            return (si % 2) * 16384

        def slot_key(si):
            return "sA" if si % 2 == 0 else "sB"

        def emit_loads(si):
            st = steps[si]
            base = slot_base(si)
            sk = slot_key(si)
            dwS = "dwA" if si % 2 == 0 else "dwB"
            if st[0] == "A":
                for h_ in range(2):
                    dma("g", dwS, wc(base + h_ * 4096, 4096),
                        wk_d[h_ * 512:(h_ + 1) * 512, :].rearrange(
                            "(c p) d -> p c d", p=128),
                        (), (sk,), f"ld_wk{h_}")
                for h_ in range(2):
                    dma("g", dwS, wc(base + 8192 + h_ * 4096, 4096),
                        wv_d[h_ * 512:(h_ + 1) * 512, :].rearrange(
                            "(c p) d -> p c d", p=128),
                        (), (sk,), f"ld_wv{h_}")
                dma("g", "dcr", crow[0:1, 0:1024], bk_d[None, :], (), ("crow",), "ld_bk")
                dma("g", "dcr", crow[0:1, 1024:2048], bv_d[None, :], (), ("crow",), "ld_bv")
            elif st[0] == "B":
                _, ex, hh = st
                dma("g", dwS, wc(base, 8192),
                    w1_d[ex, :, hh * 1024:(hh + 1) * 1024].rearrange(
                        "(c p) h -> p c h", p=128),
                    (), (sk,), f"ld_w1_{ex}_{hh}")
                dma("g", dwS, wc(base + 8192, 8192),
                    w2_d[ex, hh * 1024:(hh + 1) * 1024, :].rearrange(
                        "(j p) d -> p j d", p=128),
                    (), (sk,), f"ld_w2_{ex}_{hh}")
                if hh == 0:
                    dma("g", "db2", b2row[0:1, :], b2_d[ex:ex + 1, :],
                        (), ("b2row",), f"ld_b2_{ex}")
            else:
                dma("g", dwS, wc(base, 8192),
                    wq_d[:, :].rearrange("(c p) d -> p c d", p=128),
                    (), (sk,), "ld_wq")
                dma("g", dwS, wc(base + 8192, 8192),
                    wo_d[:, :].rearrange("(c p) d -> p c d", p=128),
                    (), (sk,), "ld_wo")
                dma("g", "dcr", crow[0:1, 0:1024], bq_d[None, :], (), ("crow",), "ld_bq")
                dma("g", "dcr", crow[0:1, 1024:2048], bo_d[None, :], (), ("crow",), "ld_bo")

        dma("g", "dcst", ident_s[:, :], ident_d[:, :], (), ("ident",), "ld_ident")
        dma("g", "dcst", ones_s[0:1, :], ones_d[:, :], (), ("ones",), "ld_ones")
        dma("g", "dcst", lt_s[:, :], lt_d[:, :], (), ("lt",), "ld_lt")
        dma("g", "dcst", onescol_s[:, :], onescol_d[:, :], (), ("oc",), "ld_oc")
        dma("g", "dcst", ecol_s[:, :], ecols_d[:, :], (), ("ec",), "ld_ec")
        dma("s", "dcr", identf[:, :], identf_d[:, :], (), ("idf",), "ld_idf")
        for c in range(8):
            dma("g", "dcst", gate_s[:, c * 16:(c + 1) * 16],
                gate_d[c * 128:(c + 1) * 128, :], (), ("gate",), f"ld_gate{c}")
        for i in range(2):
            dma("s", f"dx{i % 2}", xin[:, (i % 2) * 1024:(i % 2) * 1024 + 1024],
                x_d[i * 128:(i + 1) * 128, :], (), (f"xin{i % 2}",), f"ld_x{i}")
        emit_loads(0)
        emit_loads(1)
        for ex_ in range(NE):
            ep_ = ex_ % 2
            dma("s", f"dbc{ep_}", b1stg[0:16, ep_ * 128:(ep_ + 1) * 128],
                b1_d[ex_, :].rearrange("(m p) -> m p", p=128),
                (), (f"b1s{ep_}",), f"ld_b1_{ex_}")
            B.op("p", (lambda e_: (lambda eng: eng.transpose(
                     ps[6][:, 0:16], b1stg[0:16, (e_ % 2) * 128:(e_ % 2 + 1) * 128],
                     identf[:, :])))(ex_),
                 reads=(f"b1s{ep_}", "idf"), writes=("ps6",), sem="sp",
                 tag=f"b1t_{ex_}")
            B.op("a", (lambda e_: (lambda eng: eng.activation(
                     b1col[:, e_ * 16:(e_ + 1) * 16], ps[6][:, 0:16],
                     AF.Copy)))(ex_),
                 reads=("ps6",), writes=("b1c",), tag=f"b1e_{ex_}")

        B.op("v", lambda eng: eng.memset(silb[:, :], 0.0),
             reads=(), writes=("silb0", "silb1"), tag="zero_silb")
        B.op("v", lambda eng: eng.memset(stats[:, 144:145], -0.5),
             reads=(), writes=("stats_c",), tag="c_half")
        B.op("v", lambda eng: eng.memset(stats[:, 145:146], 1e-5),
             reads=(), writes=("stats_e",), tag="c_eps")
        for e_ in range(NE):
            for hq in range(2):
                dma("s", "dz", oe_d[e_ * CS + CAP + hq * 128: e_ * CS + CAP + (hq + 1) * 128, :],
                    silb[:, 0:1024], ("silb0",), ("odz",), f"zero_oed{e_}_{hq}")

        B.op("p",
             lambda eng: eng.matmul(
                 ps[4][:, :], ones_s[0:1, 0:128],
                 silb[0:1, 0:512], start=True, stop=False,
                 skip_group_check=True),
             reads=("ones", "silb0"), writes=("ps4",), sem="sp", tag="zmm4")

        def emit_A1(si):
            def ln_part(i):
                p = i % 2
                xk = f"xin{p}"
                xtile = xin[:, p * 1024:p * 1024 + 1024]
                so = 72 * p
                stk = lambda n: f"st{n}_{p}"
                sl = silb[:, p * 1024:(p + 1) * 1024]
                slk = f"silb{p}"
                B.op("v", (lambda xt, o: (lambda eng: eng.reduce_sum(
                         stats[:, o:o + 1], xt, axis=AX)))(xtile, so),
                     reads=(xk,), writes=(stk(0),), tag=f"ln_sum{i}")
                B.op("v", (lambda xt, o, pp: (lambda eng: eng.scalar_tensor_tensor(
                         outt[:, pp * 1024:pp * 1024 + 1024], xt, 0.0, xt,
                         op0=ALU.add, op1=ALU.mult,
                         accum_out=stats[:, o + 1:o + 2])))(xtile, so, p),
                     reads=(xk,), writes=(f"out{p}", stk(1)), tag=f"ln_sq{i}")
                B.op("v", (lambda o: (lambda eng: eng.tensor_scalar(
                         stats[:, o + 2:o + 3], stats[:, o:o + 1],
                         1.0 / DM, None, op0=ALU.mult)))(so),
                     reads=(stk(0),), writes=(stk(2),), tag=f"ln_mean{i}")
                B.op("v", (lambda o: (lambda eng: eng.tensor_scalar(
                         stats[:, o + 3:o + 4], stats[:, o + 2:o + 3],
                         stats[:, o + 2:o + 3], -1.0,
                         op0=ALU.mult, op1=ALU.mult)))(so),
                     reads=(stk(2),), writes=(stk(3),), tag=f"ln_nmsq{i}")
                B.op("v", (lambda o: (lambda eng: eng.tensor_scalar(
                         stats[:, o + 4:o + 5], stats[:, o + 1:o + 2],
                         1.0 / DM, stats[:, o + 3:o + 4],
                         op0=ALU.mult, op1=ALU.add)))(so),
                     reads=(stk(1), stk(3)), writes=(stk(4),), tag=f"ln_var{i}")
                B.op("a", (lambda o: (lambda eng: eng.activation(
                         stats[:, o + 5:o + 6], stats[:, o + 4:o + 5], AF.Ln,
                         bias=stats[:, 145:146])))(so),
                     reads=(stk(4), "stats_e"), writes=(stk(5),), tag=f"ln_ln{i}")
                B.op("a", (lambda o: (lambda eng: eng.activation(
                         stats[:, o + 6:o + 7], stats[:, o + 5:o + 6], AF.Exp,
                         scale=stats[:, 144:145])))(so),
                     reads=(stk(5), "stats_c"), writes=(stk(6),), tag=f"ln_exp{i}")
                B.op("v", (lambda xt, o, s_: (lambda eng: eng.tensor_scalar(
                         s_, xt, stats[:, o + 2:o + 3], stats[:, o + 6:o + 7],
                         op0=ALU.subtract, op1=ALU.mult)))(xtile, so, sl),
                     reads=(xk, stk(2), stk(6)), writes=(slk,), tag=f"ln_xn{i}")
                for c in range(8):
                    bank = 7 if c % 2 == 0 else 6
                    bview = (ps7b if bank == 7 else ps6b)[:, 0:128]
                    B.op("p", (lambda bv, cc, s_: (lambda eng: eng.transpose(
                             bv, s_[:, cc * 128:(cc + 1) * 128], ident_s[:, :])))(bview, c, sl),
                         reads=(slk, "ident"), writes=(f"ps{bank}",), sem="sp",
                         tag=f"tp{i}_{c}")
                    B.op("a", (lambda bv, cc, ii: (lambda eng: eng.activation(
                             xnf[:, cc * TOK + ii * 128: cc * TOK + ii * 128 + 128],
                             bv, AF.Copy)))(bview, c, i),
                         reads=(f"ps{bank}",), writes=(f"xnf{i}",), tag=f"tpe{i}_{c}")
                if i + 2 < NB:
                    j = i + 2
                    dma("s", f"dx{j % 2}", xin[:, (j % 2) * 1024:(j % 2) * 1024 + 1024],
                        x_d[j * 128:(j + 1) * 128, :], (), (f"xin{j % 2}",), f"ld_x{j}")

            def logits_part(i):
                for c in range(8):
                    B.op("p", (lambda cc, ii: (lambda eng: eng.matmul(
                             ps[5][:, 0:16],
                             xnf[:, cc * TOK + ii * 128: cc * TOK + ii * 128 + 128],
                             gate_s[:, cc * 16:(cc + 1) * 16],
                             start=(cc == 0), stop=(cc == 7))))(c, i),
                         reads=(f"xnf{i}", "gate"), writes=("ps5l",),
                         sem=("sp" if c == 7 else None), tag=f"lg{i}_{c}")

            def gate_part(i):
                p = i % 2
                so = 72 * p
                B.op("v", (lambda ii: (lambda eng: eng.tensor_copy(
                         lgall[:, ii * 16:(ii + 1) * 16], ps[5][:, 0:16])))(i),
                     reads=("ps5l",), writes=("lga",), tag=f"lge{i}")
                B.op("v", (lambda o, ii: (lambda eng: eng.max(
                         out=stats[:, o + 8:o + 16],
                         in_=lgall[:, ii * 16:(ii + 1) * 16])))(so, i),
                     reads=("lga",), writes=(f"st_m8{p}",), tag=f"g_max{i}")
                B.op("v", (lambda o, ii: (lambda eng: eng.tensor_scalar(
                         mkall[:, ii * 16:(ii + 1) * 16], lgall[:, ii * 16:(ii + 1) * 16],
                         stats[:, o + 10:o + 11], None, op0=ALU.is_ge)))(so, i),
                     reads=("lga", f"st_m8{p}"), writes=("mka",), tag=f"g_mask{i}")
                B.op("p", (lambda ii: (lambda eng: eng.matmul(
                         ps[5][0:16, 16 + ii:17 + ii],
                         mkall[:, ii * 16:(ii + 1) * 16], onescol_s[:, 0:1],
                         start=True, stop=True)))(i),
                     reads=("mka", "oc"), writes=(f"ps5c{i % 4}",),
                     sem="sp", tag=f"cnt{i}")

            for i in range(NB):
                ln_part(i)
                if i >= 1:
                    gate_part(i - 1)
                logits_part(i)
            gate_part(NB - 1)

        def emit_base():
            B.op("v", lambda eng: eng.tensor_copy(kv[0:16, 0:32],
                                                  ps[5][0:16, 16:48]),
                 reads=("ps5c0", "ps5c1", "ps5c2", "ps5c3"), writes=("kvp0",),
                 tag="cnt_ev")
            B.op("p", lambda eng: eng.transpose(
                     ps7b[0:32, 0:16], kv[0:16, 0:32], ident_s[0:16, 0:16]),
                 reads=("kvp0", "ident"), writes=("ps7",), sem="sp", tag="cnt_tp")
            B.op("v", lambda eng: eng.tensor_copy(postm[0:32, 0:16], ps7b[0:32, 0:16]),
                 reads=("ps7",), writes=("postm",), tag="cnt_tpe")
            dma("s", "dsc", cnt_d[:, :], postm[0:32, 0:16], ("postm",), ("cntd",),
                "cnt_st")
            baser = outt
            baseb = silb
            B.op("v", lambda eng: eng.memset(baser[0:1, 0:16], 0.0),
                 reads=(), writes=("bs0", "out0"), tag="bz")
            dma("s", "dsc", baser[0:1, 16:512],
                cnt_d[:, :].rearrange("a b -> (a b)")[None, 0:496], ("cntd",),
                ("bs0", "out0"), "cnt_ld")
            cur, nxt = 0, 1024
            curk, nxtk = "bs0", "bs1"
            curo, nxto = "out0", "out1"
            for r in range(5):
                sh = 16 * (2 ** r)
                B.op("v", (lambda c_, n_, s_: (lambda eng: eng.tensor_copy(
                         baser[0:1, n_:n_ + s_], baser[0:1, c_:c_ + s_])))(cur, nxt, sh),
                     reads=(curk,), writes=(nxtk, nxto), tag=f"bc{r}")
                B.op("v", (lambda c_, n_, s_: (lambda eng: eng.tensor_add(
                         baser[0:1, n_ + s_:n_ + 512], baser[0:1, c_ + s_:c_ + 512],
                         baser[0:1, c_:c_ + 512 - s_])))(cur, nxt, sh),
                     reads=(curk,), writes=(nxtk, nxto), tag=f"ba{r}")
                cur, nxt = nxt, cur
                curk, nxtk = nxtk, curk
                curo, nxto = nxto, curo
            B.op("v", (lambda c_: (lambda eng: eng.tensor_copy(
                     baseb[0:1, 0:512], baser[0:1, c_:c_ + 512])))(cur),
                 reads=(curk,), writes=("bb", "silb0"), tag="b_a")
            B.op("v", (lambda c_, n_: (lambda eng: eng.tensor_sub(
                     baser[0:1, n_:n_ + 512], baser[0:1, c_:c_ + 512],
                     baseb[0:1, 0:512])))(cur, nxt),
                 reads=(curk, "bb"), writes=(nxtk, nxto), tag="b_d")
            B.op("v", (lambda n_: (lambda eng: eng.tensor_copy(
                     baseb[0:1, 512:1024], baser[0:1, n_:n_ + 512])))(nxt),
                 reads=(nxtk, "bb"), writes=("bb", "silb0"), tag="b_b")

        def emit_A2(si):
            base = slot_base(si)
            sk = slot_key(si)
            for i in range(NB):
                p = i % 2
                so = 72 * p
                pk = "postm"
                pslice = postm[:, p * 16:(p + 1) * 16]
                for kvi, w0 in enumerate([0, 8192]):
                    for dh in range(2):
                        bank = kvi * 2 + dh
                        for c in range(8):
                            B.op("p", (lambda bk_, cc, ww, dd, ii: (lambda eng: eng.matmul(
                                     ps[bk_][:, :],
                                     xnf[:, cc * TOK + ii * 128: cc * TOK + ii * 128 + 128],
                                     wc(base + ww + cc * 1024 + dd * 512, 512),
                                     start=(cc == 0), stop=False)))(bank, c, w0, dh, i),
                                 reads=(f"xnf{i}", sk), writes=(f"ps{bank}",),
                                 sem=None, tag=f"kv{i}_{kvi}_{dh}_{c}")
                        B.op("p", (lambda bk_, kk_, dd: (lambda eng: eng.matmul(
                                 ps[bk_][:, :], ones_s[0:1, 0:128],
                                 crow[0:1, kk_ * 1024 + dd * 512: kk_ * 1024 + dd * 512 + 512],
                                 start=False, stop=True)))(bank, kvi, dh),
                             reads=("ones", "crow"), writes=(f"ps{bank}",), sem="sp",
                             tag=f"kvb{i}_{kvi}_{dh}")
                        B.op("a", (lambda bk_, kk_, dd: (lambda eng: eng.activation(
                                 kv[:, kk_ * 1024 + dd * 512: kk_ * 1024 + dd * 512 + 512],
                                 ps[bk_][:, :], AF.Copy)))(bank, kvi, dh),
                             reads=(f"ps{bank}",), writes=(f"kvp{kvi}",),
                             tag=f"kve{i}_{kvi}_{dh}")
                for h in range(NH):
                    p0 = (h % 2) * 64
                    off = (h // 2) * 64
                    B.op("p", (lambda pp, oo, hh_, ii: (lambda eng: eng.matmul(
                             ps[4][pp:pp + 64, oo:oo + 64],
                             kv[:, 1024 + hh_ * 64: 1024 + (hh_ + 1) * 64],
                             kv[:, hh_ * 64: (hh_ + 1) * 64],
                             start=False, stop=(ii == NB - 1),
                             skip_group_check=True)))(p0, off, h, i),
                         reads=("kvp0", "kvp1"), writes=("ps4",),
                         sem=("sp" if h == NH - 1 else None), tag=f"sp{i}_{h}")

                B.op("p", (lambda ii: (lambda eng: eng.matmul(
                         ps[5][:, 0:16], lt_s[:, :],
                         mkall[:, ii * 16:(ii + 1) * 16],
                         start=True, stop=False)))(i),
                     reads=("lt", "mka"), writes=("ps5l",), sem=None,
                     tag=f"pos{i}")
                B.op("p", (lambda ii: (lambda eng: eng.matmul(
                         ps[5][:, 0:16], ones_s[0:1, 0:128],
                         silb[0:1, ii * 16:(ii + 1) * 16],
                         start=False, stop=False)))(i),
                     reads=("ones", "bb"), writes=("ps5l",), sem=None,
                     tag=f"poshi{i}")
                B.op("p", (lambda ii: (lambda eng: eng.matmul(
                         ps[5][:, 0:16], ones_s[0:1, 0:128],
                         silb[0:1, 512 + ii * 16: 512 + (ii + 1) * 16],
                         start=False, stop=True)))(i),
                     reads=("ones", "bb"), writes=("ps5l",), sem="sp",
                     tag=f"poslo{i}")
                B.op("a", (lambda ps_: (lambda eng: eng.activation(
                         ps_, ps[5][:, 0:16], AF.Copy)))(pslice),
                     reads=("ps5l",), writes=(pk,), tag=f"pose{i}")
                B.op("v", (lambda o, ii: (lambda eng: eng.max(
                         out=stats[:, o + 8:o + 16],
                         in_=lgall[:, ii * 16:(ii + 1) * 16])))(so, i),
                     reads=("lga",), writes=(f"st_m8{p}",), tag=f"m_max{i}")
                B.op("v", (lambda o: (lambda eng: eng.tensor_scalar(
                         stats[:, o + 64:o + 65], stats[:, o + 8:o + 9],
                         -1.0, None, op0=ALU.mult)))(so),
                     reads=(f"st_m8{p}",), writes=(f"st_nm1{p}",), tag=f"g_nm{i}")
                B.op("a", (lambda o, ii: (lambda eng: eng.activation(
                         stats[:, o + 48:o + 64], lgall[:, ii * 16:(ii + 1) * 16],
                         AF.Exp, bias=stats[:, o + 64:o + 65])))(so, i),
                     reads=("lga", f"st_nm1{p}"), writes=(f"st_ex{p}",),
                     tag=f"g_exp{i}")
                B.op("v", (lambda o, ii: (lambda eng: eng.tensor_mul(
                         stats[:, o + 48:o + 64], stats[:, o + 48:o + 64],
                         mkall[:, ii * 16:(ii + 1) * 16])))(so, i),
                     reads=(f"st_ex{p}", "mka"), writes=(f"st_ex{p}",),
                     tag=f"g_em{i}")
                B.op("v", (lambda o: (lambda eng: eng.reduce_sum(
                         stats[:, o + 65:o + 66], stats[:, o + 48:o + 64], axis=AX)))(so),
                     reads=(f"st_ex{p}",), writes=(f"st_ssum{p}",), tag=f"g_sum{i}")
                B.op("v", (lambda o: (lambda eng: eng.reciprocal(
                         stats[:, o + 66:o + 67], stats[:, o + 65:o + 66])))(so),
                     reads=(f"st_ssum{p}",), writes=(f"st_rec{p}",), tag=f"g_rec{i}")
                B.op("v", (lambda o, ii: (lambda eng: eng.tensor_scalar(
                         gall[:, ii * 16:(ii + 1) * 16], stats[:, o + 48:o + 64],
                         stats[:, o + 66:o + 67], None, op0=ALU.mult)))(so, i),
                     reads=(f"st_ex{p}", f"st_rec{p}"), writes=("gall",),
                     tag=f"g_out{i}")

                for c in range(8):
                    tb_ = 7 if c % 2 == 0 else 6
                    bview = (ps7b if tb_ == 7 else ps6b)[:, 0:128]
                    B.op("p", (lambda bv, cc, ii: (lambda eng: eng.transpose(
                             bv, xnf[:, cc * TOK + ii * 128: cc * TOK + ii * 128 + 128],
                             ident_s[:, :])))(bview, c, i),
                         reads=(f"xnf{i}", "ident"), writes=(f"ps{tb_}",),
                         sem="sp", tag=f"mtp{i}_{c}")
                    B.op("a", (lambda bv, cc, pp: (lambda eng: eng.activation(
                             kv2[:, pp * 1024 + cc * 128: pp * 1024 + (cc + 1) * 128],
                             bv, AF.Copy)))(bview, c, p),
                         reads=(f"ps{tb_}",), writes=(f"kvp{2 + p}",), tag=f"mtpe{i}_{c}")

                for s in range(3):
                    B.op("v", (lambda o, ii, ss: (lambda eng: eng.tensor_scalar(
                             stats[:, o + 32:o + 48], lgall[:, ii * 16:(ii + 1) * 16],
                             stats[:, o + 8 + ss:o + 9 + ss], None,
                             op0=ALU.is_equal)))(so, i, s),
                         reads=("lga", f"st_m8{p}"), writes=(f"sel{p}",),
                         tag=f"m_sel{i}_{s}")
                    B.op("v", (lambda o: (lambda eng: eng.scalar_tensor_tensor(
                             stats[:, o + 16:o + 32], stats[:, o + 32:o + 48], 1.0,
                             ecol_s[:, :], op0=ALU.mult, op1=ALU.mult,
                             accum_out=stats[:, o + 67:o + 68])))(so),
                         reads=(f"sel{p}", "ec"), writes=(f"dmp{p}", f"ec{p}"),
                         tag=f"m_e{i}_{s}")
                    B.op("v", (lambda o, ps_: (lambda eng: eng.scalar_tensor_tensor(
                             stats[:, o + 16:o + 32], stats[:, o + 32:o + 48], 1.0,
                             ps_, op0=ALU.mult, op1=ALU.mult,
                             accum_out=stats[:, o + 68:o + 69])))(so, pslice),
                         reads=(f"sel{p}", pk), writes=(f"dmp{p}", f"pc{p}"),
                         tag=f"m_p{i}_{s}")
                    B.op("v", (lambda o, ii, ss: (lambda eng: eng.scalar_tensor_tensor(
                             stats[:, o + 16:o + 32], stats[:, o + 32:o + 48], 1.0,
                             gall[:, ii * 16:(ii + 1) * 16], op0=ALU.mult,
                             op1=ALU.mult,
                             accum_out=gsl[:, ii * 3 + ss:ii * 3 + ss + 1])))(so, i, s),
                         reads=(f"sel{p}", "gall"), writes=(f"dmp{p}", "gsl"),
                         tag=f"m_g{i}_{s}")
                    B.op("v", (lambda o: (lambda eng: eng.tensor_scalar(
                             stats[:, o + 7:o + 8], stats[:, o + 67:o + 68],
                             float(CS), stats[:, o + 68:o + 69],
                             op0=ALU.mult, op1=ALU.add)))(so),
                         reads=(f"ec{p}", f"pc{p}"), writes=(f"df{p}",),
                         tag=f"m_d{i}_{s}")
                    B.op("a", (lambda o, ii, ss: (lambda eng: eng.activation(
                             dsti[:, ii * 3 + ss:ii * 3 + ss + 1],
                             stats[:, o + 7:o + 8], AF.Copy)))(so, i, s),
                         reads=(f"df{p}",), writes=("dsti",), tag=f"m_ci{i}_{s}")
                    B.op("g", (lambda ii, ss, pp: (lambda eng: eng.indirect_dma_start(
                             xe_d[:, :],
                             IndirectOffsetOnAxis(dsti[:, ii * 3 + ss:ii * 3 + ss + 1], 0),
                             kv2[:, pp * 1024:(pp + 1) * 1024], None)))(i, s, p),
                         reads=("dsti", f"kvp{2 + p}"), writes=("xed",), sem="dsc",
                         amt=16, tag=f"m_sc{i}_{s}")

            B.op("v", lambda eng: eng.tensor_scalar(
                     spsb[:, :], ps[4][:, :], SCALE, None, op0=ALU.mult),
                 reads=("ps4",), writes=("spsb",), tag="spe")

        def emit_B(si):
            _, ex, hh = steps[si]
            base = slot_base(si)
            sk = slot_key(si)
            ep = ex % 2

            if hh == 0:
                for blk in range(8):
                    bp = blk % 4
                    for c in range(8):
                        tb_ = 7 if c % 2 == 0 else 6
                        bview = (ps7b if tb_ == 7 else ps6b)[:, 0:128]
                        B.op("p", (lambda bv, cc, bb_, bpp: (lambda eng: eng.transpose(
                                 bv, kvt(bpp)[0][:, kvt(bpp)[1] + cc * 128: kvt(bpp)[1] + (cc + 1) * 128],
                                 ident_s[:, :])))(bview, c, blk, bp),
                             reads=(f"kvp{bp}", "ident"), writes=(f"ps{tb_}",),
                             sem="sp", tag=f"xt_{ex}_{blk}_{c}")
                        if c % 2 == 0:
                            B.op("a", (lambda bv, cc, bb_: (lambda eng: eng.activation(
                                     xef[:, cc * 1024 + bb_ * 128: cc * 1024 + bb_ * 128 + 128],
                                     bv, AF.Copy)))(bview, c, blk),
                                 reads=(f"ps{tb_}",), writes=("xef",), tag=f"xte_{ex}_{blk}_{c}")
                        else:
                            B.op("v", (lambda bv, cc, bb_: (lambda eng: eng.tensor_copy(
                                     xef[:, cc * 1024 + bb_ * 128: cc * 1024 + bb_ * 128 + 128],
                                     bv)))(bview, c, blk),
                                 reads=(f"ps{tb_}",), writes=("xef",), tag=f"xte_{ex}_{blk}_{c}")
                    if blk + 4 < 8:
                        emit_xe_blk(ex, blk + 4)

            for tt in range(2):
                for m in range(8):
                    bank = m % 4
                    for c in range(8):
                        B.op("p", (lambda bk_, cc, mm, t_: (lambda eng: eng.matmul(
                                 ps[bk_][:, :],
                                 wc(base + cc * 1024 + mm * 128, 128),
                                 xef[:, cc * 1024 + t_ * 512: cc * 1024 + (t_ + 1) * 512],
                                 start=(cc == 0), stop=(cc == 7))))(bank, c, m, tt),
                             reads=(sk, "xef"), writes=(f"ps{bank}",),
                             sem=("sp" if c == 7 else None),
                             tag=f"w1_{ex}_{hh}_{tt}_{m}_{c}")
                    B.op("a", (lambda bk_, mm, e_: (lambda eng: eng.activation(
                             hidq[:, mm * 512:(mm + 1) * 512], ps[bk_][:, :],
                             AF.Relu, bias=b1col[:, e_ * 16 + hh * 8 + mm:
                                                 e_ * 16 + hh * 8 + mm + 1])))(bank, m, ex),
                         reads=(f"ps{bank}", "b1c"), writes=(f"hid{m}",),
                         tag=f"w1e_{ex}_{hh}_{tt}_{m}")
                for tb in range(4):
                    blk = tt * 4 + tb
                    for dh in range(2):
                        bank = 4 + (tb * 2 + dh + 2) % 4
                        for j in range(8):
                            last = (j == 7 and hh == 1)
                            B.op("p", (lambda bk_, jj, t_, dd, ll: (lambda eng: eng.matmul(
                                     ps[bk_][:, :],
                                     hidq[:, jj * 512 + t_ * 128: jj * 512 + (t_ + 1) * 128],
                                     wc(base + 8192 + jj * 1024 + dd * 512, 512),
                                     start=(jj == 0), stop=ll)))(bank, j, tb, dh, last),
                                 reads=(f"hid{j}", sk), writes=(f"ps{bank}",),
                                 sem=("sp" if last else None),
                                 tag=f"w2_{ex}_{hh}_{tt}_{tb}_{dh}_{j}")
                        bp = blk % 2
                        if hh == 0:
                            B.op("p", (lambda bk_, dd: (lambda eng: eng.matmul(
                                     ps[bk_][:, :], ones_s[0:1, 0:128],
                                     b2row[0:1, dd * 512:(dd + 1) * 512],
                                     start=False, stop=True)))(bank, dh),
                                 reads=("ones", "b2row"), writes=(f"ps{bank}",),
                                 sem="sp", tag=f"w2b_{ex}_{hh}_{tt}_{tb}_{dh}")
                        B.op("v", (lambda bk_, bpp, dd: (lambda eng: eng.tensor_copy(
                                 outsb[:, bpp * 1024 + dd * 512: bpp * 1024 + dd * 512 + 512],
                                 ps[bk_][:, :])))(bank, bp, dh),
                             reads=(f"ps{bank}",), writes=(f"ob{bp}",),
                             tag=f"obe_{ex}_{hh}_{tt}_{tb}_{dh}")
                    bp = blk % 2
                    if hh == 0:
                        B.op("s", (lambda bb_, e_, bpp: (lambda eng: eng.dma_start(
                                 oe_d[e_ * CS + bb_ * 128: e_ * CS + (bb_ + 1) * 128, :],
                                 outsb[:, bpp * 1024:(bpp + 1) * 1024])))(blk, ex, bp),
                             reads=(f"ob{bp}",), writes=(f"od{ex}_{blk}",), sem="dh0",
                             amt=16, tag=f"ost_{ex}_{hh}_{blk}")
                    else:
                        B.op("g", (lambda bb_, e_, bpp: (lambda eng: eng.dma_start(
                                 oe_d[e_ * CS + bb_ * 128: e_ * CS + (bb_ + 1) * 128, :],
                                 outsb[:, bpp * 1024:(bpp + 1) * 1024],
                                 accum_op=ALU.add)))(blk, ex, bp),
                             reads=(f"ob{bp}",), writes=(f"od{ex}_{blk}",), sem="dh1",
                             amt=16, tag=f"ost_{ex}_{hh}_{blk}")

        def emit_C(si):
            base = slot_base(si)
            sk = slot_key(si)

            for c in range(8):
                for dh in range(2):
                    bank = (c * 2 + dh) % 4
                    for sub in range(2):
                        B.op("p", (lambda bk_, cc, dd, ss: (lambda eng: eng.matmul(
                                 ps[bk_][ss * 64:(ss + 1) * 64, :],
                                 spsb[ss * 64:(ss + 1) * 64, cc * 64:(cc + 1) * 64],
                                 wc(base + 8192 + cc * 1024 + dd * 512, 512)[ss * 64:(ss + 1) * 64, :],
                                 start=True, stop=True)))(bank, c, dh, sub),
                             reads=("spsb", sk), writes=(f"ps{bank}",),
                             sem=("sp" if sub == 1 else None),
                             tag=f"T_{c}_{dh}_{sub}")
                    B.op("a", (lambda bk_, cc, dd: (lambda eng: eng.activation(
                             wc(base + 8192 + cc * 1024 + dd * 512, 512),
                             ps[bk_][:, :], AF.Copy)))(bank, c, dh),
                         reads=(f"ps{bank}",), writes=(sk,), tag=f"Te_{c}_{dh}")

            for gt in range(NTT):
                xnfk = tuple(f"xnf{bb}" for bb in range(gt * 4, gt * 4 + 4))
                for co in range(8):
                    bank = co % 4
                    for c in range(8):
                        B.op("p", (lambda bk_, cc, oo, g_: (lambda eng: eng.matmul(
                                 ps[bk_][:, :],
                                 wc(base + cc * 1024 + oo * 128, 128),
                                 xnf[:, cc * TOK + g_ * 512: cc * TOK + (g_ + 1) * 512],
                                 start=(cc == 0), stop=False)))(bank, c, co, gt),
                             reads=(sk,) + xnfk, writes=(f"ps{bank}",), sem=None,
                             tag=f"q_{gt}_{co}_{c}")
                    B.op("p", (lambda bk_, oo: (lambda eng: eng.matmul(
                             ps[bk_][:, :], crow[0:1, oo * 128:(oo + 1) * 128],
                             ones_s[0:1, 0:512], start=False, stop=True)))(bank, co),
                         reads=("crow", "ones"), writes=(f"ps{bank}",), sem="sp",
                         tag=f"qb_{gt}_{co}")
                    B.op("a", (lambda bk_, oo: (lambda eng: eng.activation(
                             hidq[:, oo * 512:(oo + 1) * 512], ps[bk_][:, :],
                             AF.Copy)))(bank, co),
                         reads=(f"ps{bank}",), writes=(f"hid{co}",),
                         tag=f"qe_{gt}_{co}")

                for tb in range(4):
                    bi = gt * 4 + tb
                    p = bi % 2
                    xk = f"xin{p}"
                    if tb == 0 and gt == 0:
                        for j in (0, 1):
                            dma("s", f"dx{j % 2}",
                                xin[:, (j % 2) * 1024:(j % 2) * 1024 + 1024],
                                x_d[j * 128:(j + 1) * 128, :], (), (f"xin{j % 2}",),
                                f"ld_xc{j}")
                    for s in range(3):
                        B.op("g", (lambda ii, ss, pp: (lambda eng: eng.indirect_dma_start(
                                 xef[:, pp * 4096 + ss * 1024: pp * 4096 + (ss + 1) * 1024],
                                 None, oe_d[:, :],
                                 IndirectOffsetOnAxis(
                                     dsti[:, ii * 3 + ss:ii * 3 + ss + 1], 0))))(bi, s, p),
                             reads=("dsti", "odz", f"gth{p}")
                                   + tuple(f"od{e_}_{b_}" for e_ in range(NE)
                                           for b_ in range(8)),
                             writes=(f"gth{p}",),
                             sem=("dg0" if p == 0 else "dg1"), amt=16,
                             tag=f"gth_{bi}_{s}")
                    B.op("v", (lambda ii, pp: (lambda eng: eng.tensor_scalar(
                             acct[:, pp * 1024:(pp + 1) * 1024],
                             xef[:, pp * 4096: pp * 4096 + 1024],
                             gsl[:, ii * 3:ii * 3 + 1], None, op0=ALU.mult)))(bi, p),
                         reads=(f"gth{p}", "gsl"), writes=(f"acct{p}",),
                         tag=f"cmb0_{bi}")
                    for s in (1, 2):
                        B.op("v", (lambda ii, ss, pp: (lambda eng:
                                 eng.scalar_tensor_tensor(
                                     acct[:, pp * 1024:(pp + 1) * 1024],
                                     xef[:, pp * 4096 + ss * 1024: pp * 4096 + (ss + 1) * 1024],
                                     gsl[:, ii * 3 + ss:ii * 3 + ss + 1],
                                     acct[:, pp * 1024:(pp + 1) * 1024],
                                     op0=ALU.mult, op1=ALU.add)))(bi, s, p),
                             reads=(f"gth{p}", "gsl", f"acct{p}"), writes=(f"acct{p}",),
                             tag=f"cmb{s}_{bi}")
                    B.op("a", (lambda pp: (lambda eng: eng.activation(
                             silb[:, pp * 1024:(pp + 1) * 1024],
                             acct[:, pp * 1024:(pp + 1) * 1024], AF.Silu)))(p),
                         reads=(f"acct{p}",), writes=(f"silb{p}",), tag=f"sil_{bi}")
                    for dh in range(2):
                        bank = 4 + (tb * 2 + dh) % 4
                        for c in range(8):
                            B.op("p", (lambda bk_, cc, t_, dd: (lambda eng: eng.matmul(
                                     ps[bk_][:, :],
                                     hidq[:, cc * 512 + t_ * 128: cc * 512 + (t_ + 1) * 128],
                                     wc(base + 8192 + cc * 1024 + dd * 512, 512),
                                     start=(cc == 0), stop=False)))(bank, c, tb, dh),
                                 reads=(f"hid{c}", sk), writes=(f"ps{bank}",),
                                 sem=None, tag=f"at_{bi}_{dh}_{c}")
                        B.op("p", (lambda bk_, dd: (lambda eng: eng.matmul(
                                 ps[bk_][:, :], ones_s[0:1, 0:128],
                                 crow[0:1, 1024 + dd * 512: 1024 + (dd + 1) * 512],
                                 start=False, stop=True)))(bank, dh),
                             reads=("ones", "crow"), writes=(f"ps{bank}",), sem="sp",
                             tag=f"atb_{bi}_{dh}")
                        B.op("v", (lambda bk_, pp, dd: (lambda eng: eng.tensor_mul(
                                 outt[:, pp * 1024 + dd * 512: pp * 1024 + dd * 512 + 512],
                                 ps[bk_][:, :],
                                 silb[:, pp * 1024 + dd * 512: pp * 1024 + dd * 512 + 512])))(bank, p, dh),
                             reads=(f"ps{bank}", f"silb{p}"), writes=(f"out{p}",),
                             tag=f"om_{bi}_{dh}")
                    B.op("v", (lambda pp: (lambda eng: eng.tensor_add(
                             outt[:, pp * 1024:(pp + 1) * 1024],
                             outt[:, pp * 1024:(pp + 1) * 1024],
                             xin[:, pp * 1024:(pp + 1) * 1024])))(p),
                         reads=(f"out{p}", xk), writes=(f"out{p}",),
                         tag=f"oa_{bi}")
                    dma("s", f"dy{p}", y_d[bi * 128:(bi + 1) * 128, :],
                        outt[:, p * 1024:(p + 1) * 1024], (f"out{p}",), (),
                        f"st_y_{bi}")
                    j = bi + 2
                    if j < NB:
                        dma("s", f"dx{j % 2}",
                            xin[:, (j % 2) * 1024:(j % 2) * 1024 + 1024],
                            x_d[j * 128:(j + 1) * 128, :], (), (f"xin{j % 2}",),
                            f"ld_xc{j}")

        for si, st in enumerate(steps):
            if st[0] == "A":
                emit_A1(si)
                emit_base()
                emit_A2(si)
            elif st[0] == "B":
                emit_B(si)
            else:
                emit_C(si)
            if si + 1 < len(steps) and steps[si + 1][0] == "B" and steps[si + 1][2] == 0:
                emit_xe(steps[si + 1][1])
            if si + 2 < len(steps):
                emit_loads(si + 2)

        tot0, tot1 = B.cnt.get("dy0", 0), B.cnt.get("dy1", 0)
        B.finish()

        items = B.items

        waited = set()
        for (_, wlist, _, _, _, _) in items:
            for (s, v) in wlist:
                waited.add((s, v))
        cnt2 = {}
        kept = {}
        keptn = {}
        keep_flag = []
        for (en, wlist, fn, sem, amt, tag) in items:
            if sem is None:
                keep_flag.append(False)
                continue
            cnt2[sem] = cnt2.get(sem, 0) + amt
            v = cnt2[sem]
            if amt != 1 or (sem, v) in waited:
                keptn[sem] = keptn.get(sem, 0) + amt
                kept[(sem, v)] = keptn[sem]
                keep_flag.append(True)
            else:
                keep_flag.append(False)

        def run_engine(eng_obj, ename):
            cnt3 = {}
            for idx, (en, wlist, fn, sem, amt, tag) in enumerate(items):
                if en != ename:
                    continue
                for (s, v) in wlist:
                    eng_obj.wait_ge(sems[s], kept[(s, v)])
                inst = fn(eng_obj)
                if sem is not None and keep_flag[idx]:
                    inst.then_inc(sems[sem], amt)
            if ename == "s":
                eng_obj.wait_ge(sems["dy0"], tot0)
                eng_obj.wait_ge(sems["dy1"], tot1)

        with nc.Block() as block:
            @block.tensor
            def _(eng):
                run_engine(eng, "p")

            @block.vector
            def _(eng):
                run_engine(eng, "v")

            @block.scalar
            def _(eng):
                run_engine(eng, "a")

            @block.gpsimd
            def _(eng):
                run_engine(eng, "g")

            @block.sync
            def _(eng):
                run_engine(eng, "s")

    return nc


_NC_CACHE = None


def kernel(L0_attn_x, L0_attn_Wq, L0_attn_bq, L0_attn_Wk, L0_attn_bk,
           L0_attn_Wv, L0_attn_bv, L0_attn_Wo, L0_attn_bo,
           L0_ffn_gate, L0_ffn_W1, L0_ffn_b1, L0_ffn_W2, L0_ffn_b2):
    global _NC_CACHE
    if _NC_CACHE is None:
        _NC_CACHE = build_nc()
    nc = _NC_CACHE

    x = np.ascontiguousarray(np.asarray(L0_attn_x, dtype=np.float32))
    shared = {
        "wq": np.ascontiguousarray(L0_attn_Wq, dtype=np.float32),
        "wk": np.ascontiguousarray(L0_attn_Wk, dtype=np.float32),
        "wv": np.ascontiguousarray(L0_attn_Wv, dtype=np.float32),
        "wo": np.ascontiguousarray(L0_attn_Wo, dtype=np.float32),
        "bq": np.ascontiguousarray(L0_attn_bq, dtype=np.float32),
        "bk": np.ascontiguousarray(L0_attn_bk, dtype=np.float32),
        "bv": np.ascontiguousarray(L0_attn_bv, dtype=np.float32),
        "bo": np.ascontiguousarray(L0_attn_bo, dtype=np.float32),
        "gate": np.ascontiguousarray(L0_ffn_gate, dtype=np.float32),
        "w1": np.ascontiguousarray(L0_ffn_W1, dtype=np.float32),
        "b1": np.ascontiguousarray(L0_ffn_b1, dtype=np.float32),
        "w2": np.ascontiguousarray(L0_ffn_W2, dtype=np.float32),
        "b2": np.ascontiguousarray(L0_ffn_b2, dtype=np.float32),
    }
    in_maps = [dict(shared, x=np.ascontiguousarray(x[c])) for c in range(8)]
    res = run_bass_kernel_spmd(nc, in_maps, core_ids=list(range(8)))
    return np.stack([np.asarray(r["y"], dtype=np.float32) for r in res.results])


# version 29 (final)
# speedup vs baseline: 1.0099x; 1.0084x over previous
import numpy as np
import ml_dtypes
from contextlib import ExitStack

import concourse.bass as bass
import concourse.mybir as mybir
from concourse.bass import IndirectOffsetOnAxis
from concourse.bass_utils import run_bass_kernel_spmd

F32 = mybir.dt.float32
BF16 = mybir.dt.bfloat16
I32 = mybir.dt.int32
AF = mybir.ActivationFunctionType
ALU = mybir.AluOpType
AX = mybir.AxisListType.X

TOK = 4096
DM = 1024
HID = 2048
NE = 16
NH, HD = 16, 64
NB = TOK // 128
NTT = TOK // 512
SCALE = 0.125
CAP = 1024
CS = 1280
NROW = NE * CS


class Builder:
    ENG2SEM = {"p": "sp", "a": "sa", "v": "sv", "g": "sg", "s": "ss"}

    def __init__(self):
        self.items = []
        self.cnt = {}
        self.wm = {}
        self.reg = {}
        self.pend = {}
        self.sem_names = set(self.ENG2SEM.values())
        self.dma_sems = set()

    def op(self, eng, fn, reads=(), writes=(), sem="ENG", amt=1, tag=""):
        waits = {}

        def need(m):
            if m is None:
                return
            s, v, meng = m
            if meng == "p" and eng == "p":
                return
            assert v is not None, f"unresolved marker ({s}) needed by {tag}"
            if s in self.dma_sems:
                v = self.cnt[s]
            if waits.get(s, 0) < v:
                waits[s] = v

        for k in reads:
            st = self.reg.get(k)
            if st:
                need(st["w"])
        for k in writes:
            st = self.reg.get(k)
            if st:
                need(st["w"])
                for m in st["r"]:
                    need(m)
        wmm = self.wm.setdefault(eng, {})
        wlist = []
        for s, v in sorted(waits.items()):
            if wmm.get(s, 0) < v:
                wlist.append((s, v))
                wmm[s] = v
        if sem == "ENG":
            sem = self.ENG2SEM[eng]
        if sem is not None:
            self.sem_names.add(sem)
            if amt != 1:
                self.dma_sems.add(sem)
            self.cnt[sem] = self.cnt.get(sem, 0) + amt
            newv = self.cnt[sem]
            for m in self.pend.pop(sem, []):
                m[1] = newv
            marker = [sem, newv, eng]
        else:
            esem = self.ENG2SEM[eng]
            marker = [esem, None, eng]
            self.pend.setdefault(esem, []).append(marker)
        self.items.append((eng, wlist, fn, sem, amt, tag))
        for k in writes:
            self.reg[k] = {"w": marker, "r": []}
        for k in reads:
            self.reg.setdefault(k, {"w": None, "r": []})["r"].append(marker)

    def finish(self):
        leftover = {k: v for k, v in self.pend.items() if v}
        assert not leftover, f"pending markers never resolved: {list(leftover)}"


def build_nc():
    nc = bass.Bass("TRN2", target_bir_lowering=False)

    x_d = nc.dram_tensor("x", [TOK, DM], F32, kind="ExternalInput")
    wq_d = nc.dram_tensor("wq", [DM, DM], F32, kind="ExternalInput")
    wk_d = nc.dram_tensor("wk", [DM, DM], F32, kind="ExternalInput")
    wv_d = nc.dram_tensor("wv", [DM, DM], F32, kind="ExternalInput")
    wo_d = nc.dram_tensor("wo", [DM, DM], F32, kind="ExternalInput")
    bq_d = nc.dram_tensor("bq", [DM], F32, kind="ExternalInput")
    bk_d = nc.dram_tensor("bk", [DM], F32, kind="ExternalInput")
    bv_d = nc.dram_tensor("bv", [DM], F32, kind="ExternalInput")
    bo_d = nc.dram_tensor("bo", [DM], F32, kind="ExternalInput")
    gate_d = nc.dram_tensor("gate", [DM, NE], F32, kind="ExternalInput")
    w1_d = nc.dram_tensor("w1", [NE, DM, HID], F32, kind="ExternalInput")
    b1_d = nc.dram_tensor("b1", [NE, HID], F32, kind="ExternalInput")
    w2_d = nc.dram_tensor("w2", [NE, HID, DM], F32, kind="ExternalInput")
    b2_d = nc.dram_tensor("b2", [NE, DM], F32, kind="ExternalInput")
    y_d = nc.dram_tensor("y", [TOK, DM], F32, kind="ExternalOutput")

    xe_d = nc.dram_tensor("xed", [NROW, DM], BF16, kind="Internal")
    oe_d = nc.dram_tensor("oed", [NROW, DM], BF16, kind="Internal")
    cnt_d = nc.dram_tensor("cntd", [NB, NE], F32, kind="Internal")

    ident_d = nc.inline_tensor(np.eye(128, dtype=ml_dtypes.bfloat16), "identc")
    ones_d = nc.inline_tensor(np.ones((1, 512), dtype=ml_dtypes.bfloat16), "onesc")
    lt_np = np.triu(np.ones((128, 128), dtype=np.float32), 1).astype(
        ml_dtypes.bfloat16)
    lt_d = nc.inline_tensor(np.ascontiguousarray(lt_np), "ltc")
    onescol_d = nc.inline_tensor(np.ones((128, 1), dtype=ml_dtypes.bfloat16), "occ")
    identf_d = nc.inline_tensor(np.eye(16, dtype=np.float32), "idfc")
    ecols_d = nc.inline_tensor(
        np.broadcast_to(np.arange(16, dtype=np.float32), (128, 16)).copy(), "ecc")

    B = Builder()

    with ExitStack() as ctx:
        e = ctx.enter_context
        xnf = e(nc.sbuf_tensor("xnf", [128, 8 * TOK], BF16))
        wbuf = e(nc.sbuf_tensor("wbuf", [128, 32768], BF16))
        xef = e(nc.sbuf_tensor("xef", [128, 8192], BF16))
        hidq = e(nc.sbuf_tensor("hidq", [128, 8 * 512], BF16))
        kv = e(nc.sbuf_tensor("kv", [128, 2048], BF16))
        kv2 = e(nc.sbuf_tensor("kv2", [128, 2048], BF16))
        kvt = lambda pp: (kv if pp < 2 else kv2, (pp % 2) * 1024)
        xin = e(nc.sbuf_tensor("xin", [128, 2048], F32))
        outt = e(nc.sbuf_tensor("outt", [128, 2048], F32))
        silb = e(nc.sbuf_tensor("silb", [128, 2048], BF16))
        outsb = silb
        acct = e(nc.sbuf_tensor("acct", [128, 2048], F32))
        spsb = e(nc.sbuf_tensor("spsb", [128, 512], BF16))
        gall = e(nc.sbuf_tensor("gall", [128, NB * NE], F32))
        lgall = e(nc.sbuf_tensor("lgall", [128, NB * NE], F32))
        mkall = e(nc.sbuf_tensor("mkall", [128, NB * NE], BF16))
        dsti = e(nc.sbuf_tensor("dsti", [128, NB * 3], I32))
        gsl = e(nc.sbuf_tensor("gsl", [128, NB * 3], F32))
        postm = e(nc.sbuf_tensor("postm", [128, 64], F32))
        stats = e(nc.sbuf_tensor("stats", [128, 148], F32))
        crow = e(nc.sbuf_tensor("crow", [1, 2048], BF16))
        b2row = e(nc.sbuf_tensor("b2row", [1, DM], BF16))
        b1col = e(nc.sbuf_tensor("b1col", [128, 256], F32))
        b1stg = e(nc.sbuf_tensor("b1stg", [16, 256], F32))
        identf = e(nc.sbuf_tensor("identf", [16, 16], F32))
        gate_s = e(nc.sbuf_tensor("gate_s", [128, 128], BF16))
        ident_s = e(nc.sbuf_tensor("ident_s", [128, 128], BF16))
        lt_s = e(nc.sbuf_tensor("lt_s", [128, 128], BF16))
        ecol_s = e(nc.sbuf_tensor("ecol_s", [128, 16], F32))
        onescol_s = e(nc.sbuf_tensor("onescol_s", [128, 1], BF16))
        ones_s = e(nc.sbuf_tensor("ones_s", [1, 512], BF16))

        ps = [e(nc.psum_tensor(f"ps{i}", [128, 512], F32)) for i in range(8)]
        ps6b = ps[6][:, :].bitcast(BF16)
        ps7b = ps[7][:, :].bitcast(BF16)

        sems = {}
        for name in ["sp", "sa", "sv", "sg", "ss",
                     "dwA", "dwB", "dxe", "dbc0", "dbc1", "db2", "dcst",
                     "dcr", "dsc", "dg0", "dg1", "dh0", "dh1", "dz",
                     "dx0", "dx1", "dy0", "dy1"]:
            sems[name] = e(nc.semaphore(name))

        def dma(queue_eng, sem, out_ap, in_ap, reads, writes, tag):
            B.op(queue_eng,
                 (lambda o, i: (lambda eng: eng.dma_start(o, i)))(out_ap, in_ap),
                 reads=reads, writes=writes, sem=sem, amt=16, tag=tag)

        def wc(col0, ncols):
            return wbuf[:, col0:col0 + ncols]

        def emit_xe_blk(ex, blk):
            bp = blk % 4
            tt_, c0 = kvt(bp)
            dma("s", "dxe", tt_[:, c0:c0 + 1024],
                xe_d[ex * CS + blk * 128: ex * CS + (blk + 1) * 128, :],
                ("xed",), (f"kvp{bp}",), f"xe_{ex}_{blk}")

        def emit_xe(ex):
            for blk in range(4):
                emit_xe_blk(ex, blk)


        steps = [("A",)]
        for ex in range(NE):
            for hh in range(2):
                steps.append(("B", ex, hh))
        steps.append(("C",))

        def slot_base(si):
            return (si % 2) * 16384

        def slot_key(si):
            return "sA" if si % 2 == 0 else "sB"

        def emit_loads(si):
            st = steps[si]
            base = slot_base(si)
            sk = slot_key(si)
            dwS = "dwA" if si % 2 == 0 else "dwB"
            if st[0] == "A":
                for h_ in range(2):
                    dma("g", dwS, wc(base + h_ * 4096, 4096),
                        wk_d[h_ * 512:(h_ + 1) * 512, :].rearrange(
                            "(c p) d -> p c d", p=128),
                        (), (sk,), f"ld_wk{h_}")
                for h_ in range(2):
                    dma("g", dwS, wc(base + 8192 + h_ * 4096, 4096),
                        wv_d[h_ * 512:(h_ + 1) * 512, :].rearrange(
                            "(c p) d -> p c d", p=128),
                        (), (sk,), f"ld_wv{h_}")
                dma("g", "dcr", crow[0:1, 0:1024], bk_d[None, :], (), ("crow",), "ld_bk")
                dma("g", "dcr", crow[0:1, 1024:2048], bv_d[None, :], (), ("crow",), "ld_bv")
            elif st[0] == "B":
                _, ex, hh = st
                dma("g", dwS, wc(base, 8192),
                    w1_d[ex, :, hh * 1024:(hh + 1) * 1024].rearrange(
                        "(c p) h -> p c h", p=128),
                    (), (sk,), f"ld_w1_{ex}_{hh}")
                dma("g", dwS, wc(base + 8192, 8192),
                    w2_d[ex, hh * 1024:(hh + 1) * 1024, :].rearrange(
                        "(j p) d -> p j d", p=128),
                    (), (sk,), f"ld_w2_{ex}_{hh}")
                if hh == 0:
                    dma("g", "db2", b2row[0:1, :], b2_d[ex:ex + 1, :],
                        (), ("b2row",), f"ld_b2_{ex}")
            else:
                dma("g", dwS, wc(base, 8192),
                    wq_d[:, :].rearrange("(c p) d -> p c d", p=128),
                    (), (sk,), "ld_wq")
                dma("g", dwS, wc(base + 8192, 8192),
                    wo_d[:, :].rearrange("(c p) d -> p c d", p=128),
                    (), (sk,), "ld_wo")
                dma("g", "dcr", crow[0:1, 0:1024], bq_d[None, :], (), ("crow",), "ld_bq")
                dma("g", "dcr", crow[0:1, 1024:2048], bo_d[None, :], (), ("crow",), "ld_bo")

        dma("g", "dcst", ident_s[:, :], ident_d[:, :], (), ("ident",), "ld_ident")
        dma("g", "dcst", ones_s[0:1, :], ones_d[:, :], (), ("ones",), "ld_ones")
        dma("g", "dcst", lt_s[:, :], lt_d[:, :], (), ("lt",), "ld_lt")
        dma("g", "dcst", onescol_s[:, :], onescol_d[:, :], (), ("oc",), "ld_oc")
        dma("g", "dcst", ecol_s[:, :], ecols_d[:, :], (), ("ec",), "ld_ec")
        dma("s", "dcr", identf[:, :], identf_d[:, :], (), ("idf",), "ld_idf")
        for c in range(8):
            dma("g", "dcst", gate_s[:, c * 16:(c + 1) * 16],
                gate_d[c * 128:(c + 1) * 128, :], (), ("gate",), f"ld_gate{c}")
        for i in range(2):
            dma("s", f"dx{i % 2}", xin[:, (i % 2) * 1024:(i % 2) * 1024 + 1024],
                x_d[i * 128:(i + 1) * 128, :], (), (f"xin{i % 2}",), f"ld_x{i}")
        emit_loads(0)
        emit_loads(1)
        for ex_ in range(NE):
            ep_ = ex_ % 2
            dma("s", f"dbc{ep_}", b1stg[0:16, ep_ * 128:(ep_ + 1) * 128],
                b1_d[ex_, :].rearrange("(m p) -> m p", p=128),
                (), (f"b1s{ep_}",), f"ld_b1_{ex_}")
            B.op("p", (lambda e_: (lambda eng: eng.transpose(
                     ps[6][:, 0:16], b1stg[0:16, (e_ % 2) * 128:(e_ % 2 + 1) * 128],
                     identf[:, :])))(ex_),
                 reads=(f"b1s{ep_}", "idf"), writes=("ps6",), sem="sp",
                 tag=f"b1t_{ex_}")
            B.op("a", (lambda e_: (lambda eng: eng.activation(
                     b1col[:, e_ * 16:(e_ + 1) * 16], ps[6][:, 0:16],
                     AF.Copy)))(ex_),
                 reads=("ps6",), writes=("b1c",), tag=f"b1e_{ex_}")

        B.op("v", lambda eng: eng.memset(silb[:, :], 0.0),
             reads=(), writes=("silb0", "silb1"), tag="zero_silb")
        B.op("v", lambda eng: eng.memset(stats[:, 144:145], -0.5),
             reads=(), writes=("stats_c",), tag="c_half")
        B.op("v", lambda eng: eng.memset(stats[:, 145:146], 1e-5),
             reads=(), writes=("stats_e",), tag="c_eps")
        for e_ in range(NE):
            for hq in range(2):
                dma("s", "dz", oe_d[e_ * CS + CAP + hq * 128: e_ * CS + CAP + (hq + 1) * 128, :],
                    silb[:, 0:1024], ("silb0",), ("odz",), f"zero_oed{e_}_{hq}")

        B.op("p",
             lambda eng: eng.matmul(
                 ps[4][:, :], ones_s[0:1, 0:128],
                 silb[0:1, 0:512], start=True, stop=False,
                 skip_group_check=True),
             reads=("ones", "silb0"), writes=("ps4",), sem="sp", tag="zmm4")

        def emit_A1(si):
            def ln_part(i):
                p = i % 2
                xk = f"xin{p}"
                xtile = xin[:, p * 1024:p * 1024 + 1024]
                so = 72 * p
                stk = lambda n: f"st{n}_{p}"
                sl = silb[:, p * 1024:(p + 1) * 1024]
                slk = f"silb{p}"
                B.op("v", (lambda xt, o: (lambda eng: eng.reduce_sum(
                         stats[:, o:o + 1], xt, axis=AX)))(xtile, so),
                     reads=(xk,), writes=(stk(0),), tag=f"ln_sum{i}")
                B.op("v", (lambda xt, o, pp: (lambda eng: eng.scalar_tensor_tensor(
                         outt[:, pp * 1024:pp * 1024 + 1024], xt, 0.0, xt,
                         op0=ALU.add, op1=ALU.mult,
                         accum_out=stats[:, o + 1:o + 2])))(xtile, so, p),
                     reads=(xk,), writes=(f"out{p}", stk(1)), tag=f"ln_sq{i}")
                B.op("v", (lambda o: (lambda eng: eng.tensor_scalar(
                         stats[:, o + 2:o + 3], stats[:, o:o + 1],
                         1.0 / DM, None, op0=ALU.mult)))(so),
                     reads=(stk(0),), writes=(stk(2),), tag=f"ln_mean{i}")
                B.op("v", (lambda o: (lambda eng: eng.tensor_scalar(
                         stats[:, o + 3:o + 4], stats[:, o + 2:o + 3],
                         stats[:, o + 2:o + 3], -1.0,
                         op0=ALU.mult, op1=ALU.mult)))(so),
                     reads=(stk(2),), writes=(stk(3),), tag=f"ln_nmsq{i}")
                B.op("v", (lambda o: (lambda eng: eng.tensor_scalar(
                         stats[:, o + 4:o + 5], stats[:, o + 1:o + 2],
                         1.0 / DM, stats[:, o + 3:o + 4],
                         op0=ALU.mult, op1=ALU.add)))(so),
                     reads=(stk(1), stk(3)), writes=(stk(4),), tag=f"ln_var{i}")
                B.op("a", (lambda o: (lambda eng: eng.activation(
                         stats[:, o + 5:o + 6], stats[:, o + 4:o + 5], AF.Ln,
                         bias=stats[:, 145:146])))(so),
                     reads=(stk(4), "stats_e"), writes=(stk(5),), tag=f"ln_ln{i}")
                B.op("a", (lambda o: (lambda eng: eng.activation(
                         stats[:, o + 6:o + 7], stats[:, o + 5:o + 6], AF.Exp,
                         scale=stats[:, 144:145])))(so),
                     reads=(stk(5), "stats_c"), writes=(stk(6),), tag=f"ln_exp{i}")
                B.op("v", (lambda xt, o, s_: (lambda eng: eng.tensor_scalar(
                         s_, xt, stats[:, o + 2:o + 3], stats[:, o + 6:o + 7],
                         op0=ALU.subtract, op1=ALU.mult)))(xtile, so, sl),
                     reads=(xk, stk(2), stk(6)), writes=(slk,), tag=f"ln_xn{i}")
                for c in range(8):
                    bank = 7 if c % 2 == 0 else 6
                    bview = (ps7b if bank == 7 else ps6b)[:, 0:128]
                    B.op("p", (lambda bv, cc, s_: (lambda eng: eng.transpose(
                             bv, s_[:, cc * 128:(cc + 1) * 128], ident_s[:, :])))(bview, c, sl),
                         reads=(slk, "ident"), writes=(f"ps{bank}",), sem="sp",
                         tag=f"tp{i}_{c}")
                    B.op("a", (lambda bv, cc, ii: (lambda eng: eng.activation(
                             xnf[:, cc * TOK + ii * 128: cc * TOK + ii * 128 + 128],
                             bv, AF.Copy)))(bview, c, i),
                         reads=(f"ps{bank}",), writes=(f"xnf{i}",), tag=f"tpe{i}_{c}")
                if i + 2 < NB:
                    j = i + 2
                    dma("s", f"dx{j % 2}", xin[:, (j % 2) * 1024:(j % 2) * 1024 + 1024],
                        x_d[j * 128:(j + 1) * 128, :], (), (f"xin{j % 2}",), f"ld_x{j}")

            def logits_part(i):
                for c in range(8):
                    B.op("p", (lambda cc, ii: (lambda eng: eng.matmul(
                             ps[5][:, 0:16],
                             xnf[:, cc * TOK + ii * 128: cc * TOK + ii * 128 + 128],
                             gate_s[:, cc * 16:(cc + 1) * 16],
                             start=(cc == 0), stop=(cc == 7))))(c, i),
                         reads=(f"xnf{i}", "gate"), writes=("ps5l",),
                         sem=("sp" if c == 7 else None), tag=f"lg{i}_{c}")

            def gate_part(i):
                p = i % 2
                so = 72 * p
                B.op("v", (lambda ii: (lambda eng: eng.tensor_copy(
                         lgall[:, ii * 16:(ii + 1) * 16], ps[5][:, 0:16])))(i),
                     reads=("ps5l",), writes=("lga",), tag=f"lge{i}")
                B.op("v", (lambda o, ii: (lambda eng: eng.max(
                         out=stats[:, o + 8:o + 16],
                         in_=lgall[:, ii * 16:(ii + 1) * 16])))(so, i),
                     reads=("lga",), writes=(f"st_m8{p}",), tag=f"g_max{i}")
                B.op("v", (lambda o, ii: (lambda eng: eng.tensor_scalar(
                         mkall[:, ii * 16:(ii + 1) * 16], lgall[:, ii * 16:(ii + 1) * 16],
                         stats[:, o + 10:o + 11], None, op0=ALU.is_ge)))(so, i),
                     reads=("lga", f"st_m8{p}"), writes=("mka",), tag=f"g_mask{i}")
                B.op("p", (lambda ii: (lambda eng: eng.matmul(
                         ps[5][0:16, 16 + ii:17 + ii],
                         mkall[:, ii * 16:(ii + 1) * 16], onescol_s[:, 0:1],
                         start=True, stop=True)))(i),
                     reads=("mka", "oc"), writes=(f"ps5c{i % 4}",),
                     sem="sp", tag=f"cnt{i}")

            for i in range(NB):
                ln_part(i)
                if i >= 1:
                    gate_part(i - 1)
                logits_part(i)
            gate_part(NB - 1)

        def emit_base():
            B.op("v", lambda eng: eng.tensor_copy(kv[0:16, 0:32],
                                                  ps[5][0:16, 16:48]),
                 reads=("ps5c0", "ps5c1", "ps5c2", "ps5c3"), writes=("kvp0",),
                 tag="cnt_ev")
            B.op("p", lambda eng: eng.transpose(
                     ps7b[0:32, 0:16], kv[0:16, 0:32], ident_s[0:16, 0:16]),
                 reads=("kvp0", "ident"), writes=("ps7",), sem="sp", tag="cnt_tp")
            B.op("v", lambda eng: eng.tensor_copy(postm[0:32, 0:16], ps7b[0:32, 0:16]),
                 reads=("ps7",), writes=("postm",), tag="cnt_tpe")
            dma("s", "dsc", cnt_d[:, :], postm[0:32, 0:16], ("postm",), ("cntd",),
                "cnt_st")
            baser = outt
            baseb = silb
            B.op("v", lambda eng: eng.memset(baser[0:1, 0:16], 0.0),
                 reads=(), writes=("bs0", "out0"), tag="bz")
            dma("s", "dsc", baser[0:1, 16:512],
                cnt_d[:, :].rearrange("a b -> (a b)")[None, 0:496], ("cntd",),
                ("bs0", "out0"), "cnt_ld")
            cur, nxt = 0, 1024
            curk, nxtk = "bs0", "bs1"
            curo, nxto = "out0", "out1"
            for r in range(5):
                sh = 16 * (2 ** r)
                B.op("v", (lambda c_, n_, s_: (lambda eng: eng.tensor_copy(
                         baser[0:1, n_:n_ + s_], baser[0:1, c_:c_ + s_])))(cur, nxt, sh),
                     reads=(curk,), writes=(nxtk, nxto), tag=f"bc{r}")
                B.op("v", (lambda c_, n_, s_: (lambda eng: eng.tensor_add(
                         baser[0:1, n_ + s_:n_ + 512], baser[0:1, c_ + s_:c_ + 512],
                         baser[0:1, c_:c_ + 512 - s_])))(cur, nxt, sh),
                     reads=(curk,), writes=(nxtk, nxto), tag=f"ba{r}")
                cur, nxt = nxt, cur
                curk, nxtk = nxtk, curk
                curo, nxto = nxto, curo
            B.op("v", (lambda c_: (lambda eng: eng.tensor_copy(
                     baseb[0:1, 0:512], baser[0:1, c_:c_ + 512])))(cur),
                 reads=(curk,), writes=("bb", "silb0"), tag="b_a")
            B.op("v", (lambda c_, n_: (lambda eng: eng.tensor_sub(
                     baser[0:1, n_:n_ + 512], baser[0:1, c_:c_ + 512],
                     baseb[0:1, 0:512])))(cur, nxt),
                 reads=(curk, "bb"), writes=(nxtk, nxto), tag="b_d")
            B.op("v", (lambda n_: (lambda eng: eng.tensor_copy(
                     baseb[0:1, 512:1024], baser[0:1, n_:n_ + 512])))(nxt),
                 reads=(nxtk, "bb"), writes=("bb", "silb0"), tag="b_b")

        def emit_A2(si):
            base = slot_base(si)
            sk = slot_key(si)
            for i in range(NB):
                p = i % 2
                so = 72 * p
                pk = "postm"
                pslice = postm[:, p * 16:(p + 1) * 16]
                for kvi, w0 in enumerate([0, 8192]):
                    for dh in range(2):
                        bank = kvi * 2 + dh
                        for c in range(8):
                            B.op("p", (lambda bk_, cc, ww, dd, ii: (lambda eng: eng.matmul(
                                     ps[bk_][:, :],
                                     xnf[:, cc * TOK + ii * 128: cc * TOK + ii * 128 + 128],
                                     wc(base + ww + cc * 1024 + dd * 512, 512),
                                     start=(cc == 0), stop=False)))(bank, c, w0, dh, i),
                                 reads=(f"xnf{i}", sk), writes=(f"ps{bank}",),
                                 sem=None, tag=f"kv{i}_{kvi}_{dh}_{c}")
                        B.op("p", (lambda bk_, kk_, dd: (lambda eng: eng.matmul(
                                 ps[bk_][:, :], ones_s[0:1, 0:128],
                                 crow[0:1, kk_ * 1024 + dd * 512: kk_ * 1024 + dd * 512 + 512],
                                 start=False, stop=True)))(bank, kvi, dh),
                             reads=("ones", "crow"), writes=(f"ps{bank}",), sem="sp",
                             tag=f"kvb{i}_{kvi}_{dh}")
                        B.op("a", (lambda bk_, kk_, dd: (lambda eng: eng.activation(
                                 kv[:, kk_ * 1024 + dd * 512: kk_ * 1024 + dd * 512 + 512],
                                 ps[bk_][:, :], AF.Copy)))(bank, kvi, dh),
                             reads=(f"ps{bank}",), writes=(f"kvp{kvi}",),
                             tag=f"kve{i}_{kvi}_{dh}")
                for h in range(NH):
                    p0 = (h % 2) * 64
                    off = (h // 2) * 64
                    B.op("p", (lambda pp, oo, hh_, ii: (lambda eng: eng.matmul(
                             ps[4][pp:pp + 64, oo:oo + 64],
                             kv[:, 1024 + hh_ * 64: 1024 + (hh_ + 1) * 64],
                             kv[:, hh_ * 64: (hh_ + 1) * 64],
                             start=False, stop=(ii == NB - 1),
                             skip_group_check=True)))(p0, off, h, i),
                         reads=("kvp0", "kvp1"), writes=("ps4",),
                         sem=("sp" if h == NH - 1 else None), tag=f"sp{i}_{h}")

                B.op("p", (lambda ii: (lambda eng: eng.matmul(
                         ps[5][:, 0:16], lt_s[:, :],
                         mkall[:, ii * 16:(ii + 1) * 16],
                         start=True, stop=False)))(i),
                     reads=("lt", "mka"), writes=("ps5l",), sem=None,
                     tag=f"pos{i}")
                B.op("p", (lambda ii: (lambda eng: eng.matmul(
                         ps[5][:, 0:16], ones_s[0:1, 0:128],
                         silb[0:1, ii * 16:(ii + 1) * 16],
                         start=False, stop=False)))(i),
                     reads=("ones", "bb"), writes=("ps5l",), sem=None,
                     tag=f"poshi{i}")
                B.op("p", (lambda ii: (lambda eng: eng.matmul(
                         ps[5][:, 0:16], ones_s[0:1, 0:128],
                         silb[0:1, 512 + ii * 16: 512 + (ii + 1) * 16],
                         start=False, stop=True)))(i),
                     reads=("ones", "bb"), writes=("ps5l",), sem="sp",
                     tag=f"poslo{i}")
                B.op("a", (lambda ps_: (lambda eng: eng.activation(
                         ps_, ps[5][:, 0:16], AF.Copy)))(pslice),
                     reads=("ps5l",), writes=(pk,), tag=f"pose{i}")
                B.op("v", (lambda o, ii: (lambda eng: eng.max(
                         out=stats[:, o + 8:o + 16],
                         in_=lgall[:, ii * 16:(ii + 1) * 16])))(so, i),
                     reads=("lga",), writes=(f"st_m8{p}",), tag=f"m_max{i}")
                B.op("v", (lambda o: (lambda eng: eng.tensor_scalar(
                         stats[:, o + 64:o + 65], stats[:, o + 8:o + 9],
                         -1.0, None, op0=ALU.mult)))(so),
                     reads=(f"st_m8{p}",), writes=(f"st_nm1{p}",), tag=f"g_nm{i}")
                B.op("a", (lambda o, ii: (lambda eng: eng.activation(
                         stats[:, o + 48:o + 64], lgall[:, ii * 16:(ii + 1) * 16],
                         AF.Exp, bias=stats[:, o + 64:o + 65])))(so, i),
                     reads=("lga", f"st_nm1{p}"), writes=(f"st_ex{p}",),
                     tag=f"g_exp{i}")
                B.op("v", (lambda o, ii: (lambda eng: eng.tensor_mul(
                         stats[:, o + 48:o + 64], stats[:, o + 48:o + 64],
                         mkall[:, ii * 16:(ii + 1) * 16])))(so, i),
                     reads=(f"st_ex{p}", "mka"), writes=(f"st_ex{p}",),
                     tag=f"g_em{i}")
                B.op("v", (lambda o: (lambda eng: eng.reduce_sum(
                         stats[:, o + 65:o + 66], stats[:, o + 48:o + 64], axis=AX)))(so),
                     reads=(f"st_ex{p}",), writes=(f"st_ssum{p}",), tag=f"g_sum{i}")
                B.op("v", (lambda o: (lambda eng: eng.reciprocal(
                         stats[:, o + 66:o + 67], stats[:, o + 65:o + 66])))(so),
                     reads=(f"st_ssum{p}",), writes=(f"st_rec{p}",), tag=f"g_rec{i}")
                B.op("v", (lambda o, ii: (lambda eng: eng.tensor_scalar(
                         gall[:, ii * 16:(ii + 1) * 16], stats[:, o + 48:o + 64],
                         stats[:, o + 66:o + 67], None, op0=ALU.mult)))(so, i),
                     reads=(f"st_ex{p}", f"st_rec{p}"), writes=("gall",),
                     tag=f"g_out{i}")

                for c in range(8):
                    tb_ = 7 if c % 2 == 0 else 6
                    bview = (ps7b if tb_ == 7 else ps6b)[:, 0:128]
                    B.op("p", (lambda bv, cc, ii: (lambda eng: eng.transpose(
                             bv, xnf[:, cc * TOK + ii * 128: cc * TOK + ii * 128 + 128],
                             ident_s[:, :])))(bview, c, i),
                         reads=(f"xnf{i}", "ident"), writes=(f"ps{tb_}",),
                         sem="sp", tag=f"mtp{i}_{c}")
                    B.op("a", (lambda bv, cc, pp: (lambda eng: eng.activation(
                             kv2[:, pp * 1024 + cc * 128: pp * 1024 + (cc + 1) * 128],
                             bv, AF.Copy)))(bview, c, p),
                         reads=(f"ps{tb_}",), writes=(f"kvp{2 + p}",), tag=f"mtpe{i}_{c}")

                for s in range(3):
                    B.op("v", (lambda o, ii, ss: (lambda eng: eng.tensor_scalar(
                             stats[:, o + 32:o + 48], lgall[:, ii * 16:(ii + 1) * 16],
                             stats[:, o + 8 + ss:o + 9 + ss], None,
                             op0=ALU.is_equal)))(so, i, s),
                         reads=("lga", f"st_m8{p}"), writes=(f"sel{p}",),
                         tag=f"m_sel{i}_{s}")
                    B.op("v", (lambda o: (lambda eng: eng.scalar_tensor_tensor(
                             stats[:, o + 16:o + 32], stats[:, o + 32:o + 48], 1.0,
                             ecol_s[:, :], op0=ALU.mult, op1=ALU.mult,
                             accum_out=stats[:, o + 67:o + 68])))(so),
                         reads=(f"sel{p}", "ec"), writes=(f"dmp{p}", f"ec{p}"),
                         tag=f"m_e{i}_{s}")
                    B.op("v", (lambda o, ps_: (lambda eng: eng.scalar_tensor_tensor(
                             stats[:, o + 16:o + 32], stats[:, o + 32:o + 48], 1.0,
                             ps_, op0=ALU.mult, op1=ALU.mult,
                             accum_out=stats[:, o + 68:o + 69])))(so, pslice),
                         reads=(f"sel{p}", pk), writes=(f"dmp{p}", f"pc{p}"),
                         tag=f"m_p{i}_{s}")
                    B.op("v", (lambda o, ii, ss: (lambda eng: eng.scalar_tensor_tensor(
                             stats[:, o + 16:o + 32], stats[:, o + 32:o + 48], 1.0,
                             gall[:, ii * 16:(ii + 1) * 16], op0=ALU.mult,
                             op1=ALU.mult,
                             accum_out=gsl[:, ii * 3 + ss:ii * 3 + ss + 1])))(so, i, s),
                         reads=(f"sel{p}", "gall"), writes=(f"dmp{p}", "gsl"),
                         tag=f"m_g{i}_{s}")
                    B.op("v", (lambda o: (lambda eng: eng.tensor_scalar(
                             stats[:, o + 7:o + 8], stats[:, o + 67:o + 68],
                             float(CS), stats[:, o + 68:o + 69],
                             op0=ALU.mult, op1=ALU.add)))(so),
                         reads=(f"ec{p}", f"pc{p}"), writes=(f"df{p}",),
                         tag=f"m_d{i}_{s}")
                    B.op("a", (lambda o, ii, ss: (lambda eng: eng.activation(
                             dsti[:, ii * 3 + ss:ii * 3 + ss + 1],
                             stats[:, o + 7:o + 8], AF.Copy)))(so, i, s),
                         reads=(f"df{p}",), writes=("dsti",), tag=f"m_ci{i}_{s}")
                    B.op("g", (lambda ii, ss, pp: (lambda eng: eng.indirect_dma_start(
                             xe_d[:, :],
                             IndirectOffsetOnAxis(dsti[:, ii * 3 + ss:ii * 3 + ss + 1], 0),
                             kv2[:, pp * 1024:(pp + 1) * 1024], None)))(i, s, p),
                         reads=("dsti", f"kvp{2 + p}"), writes=("xed",), sem="dsc",
                         amt=16, tag=f"m_sc{i}_{s}")

            B.op("v", lambda eng: eng.tensor_scalar(
                     spsb[:, :], ps[4][:, :], SCALE, None, op0=ALU.mult),
                 reads=("ps4",), writes=("spsb",), tag="spe")

        def emit_B(si):
            _, ex, hh = steps[si]
            base = slot_base(si)
            sk = slot_key(si)
            ep = ex % 2

            if hh == 0:
                for blk in range(8):
                    bp = blk % 4
                    for c in range(8):
                        tb_ = 7 if c % 2 == 0 else 6
                        bview = (ps7b if tb_ == 7 else ps6b)[:, 0:128]
                        B.op("p", (lambda bv, cc, bb_, bpp: (lambda eng: eng.transpose(
                                 bv, kvt(bpp)[0][:, kvt(bpp)[1] + cc * 128: kvt(bpp)[1] + (cc + 1) * 128],
                                 ident_s[:, :])))(bview, c, blk, bp),
                             reads=(f"kvp{bp}", "ident"), writes=(f"ps{tb_}",),
                             sem="sp", tag=f"xt_{ex}_{blk}_{c}")
                        if c % 2 == 0:
                            B.op("a", (lambda bv, cc, bb_: (lambda eng: eng.activation(
                                     xef[:, cc * 1024 + bb_ * 128: cc * 1024 + bb_ * 128 + 128],
                                     bv, AF.Copy)))(bview, c, blk),
                                 reads=(f"ps{tb_}",), writes=("xef",), tag=f"xte_{ex}_{blk}_{c}")
                        else:
                            B.op("v", (lambda bv, cc, bb_: (lambda eng: eng.tensor_copy(
                                     xef[:, cc * 1024 + bb_ * 128: cc * 1024 + bb_ * 128 + 128],
                                     bv)))(bview, c, blk),
                                 reads=(f"ps{tb_}",), writes=("xef",), tag=f"xte_{ex}_{blk}_{c}")
                    if blk + 4 < 8:
                        emit_xe_blk(ex, blk + 4)

            for tt in range(2):
                for m in range(8):
                    bank = m % 4
                    for c in range(8):
                        B.op("p", (lambda bk_, cc, mm, t_: (lambda eng: eng.matmul(
                                 ps[bk_][:, :],
                                 wc(base + cc * 1024 + mm * 128, 128),
                                 xef[:, cc * 1024 + t_ * 512: cc * 1024 + (t_ + 1) * 512],
                                 start=(cc == 0), stop=(cc == 7))))(bank, c, m, tt),
                             reads=(sk, "xef"), writes=(f"ps{bank}",),
                             sem=("sp" if c == 7 else None),
                             tag=f"w1_{ex}_{hh}_{tt}_{m}_{c}")
                    B.op("a", (lambda bk_, mm, e_: (lambda eng: eng.activation(
                             hidq[:, mm * 512:(mm + 1) * 512], ps[bk_][:, :],
                             AF.Relu, bias=b1col[:, e_ * 16 + hh * 8 + mm:
                                                 e_ * 16 + hh * 8 + mm + 1])))(bank, m, ex),
                         reads=(f"ps{bank}", "b1c"), writes=(f"hid{m}",),
                         tag=f"w1e_{ex}_{hh}_{tt}_{m}")
                for tb in range(4):
                    blk = tt * 4 + tb
                    for dh in range(2):
                        bank = 4 + (tb * 2 + dh + 2) % 4
                        for j in range(8):
                            last = (j == 7 and hh == 1)
                            B.op("p", (lambda bk_, jj, t_, dd, ll: (lambda eng: eng.matmul(
                                     ps[bk_][:, :],
                                     hidq[:, jj * 512 + t_ * 128: jj * 512 + (t_ + 1) * 128],
                                     wc(base + 8192 + jj * 1024 + dd * 512, 512),
                                     start=(jj == 0), stop=ll)))(bank, j, tb, dh, last),
                                 reads=(f"hid{j}", sk), writes=(f"ps{bank}",),
                                 sem=("sp" if last else None),
                                 tag=f"w2_{ex}_{hh}_{tt}_{tb}_{dh}_{j}")
                        bp = blk % 2
                        if hh == 0:
                            B.op("p", (lambda bk_, dd: (lambda eng: eng.matmul(
                                     ps[bk_][:, :], ones_s[0:1, 0:128],
                                     b2row[0:1, dd * 512:(dd + 1) * 512],
                                     start=False, stop=True)))(bank, dh),
                                 reads=("ones", "b2row"), writes=(f"ps{bank}",),
                                 sem="sp", tag=f"w2b_{ex}_{hh}_{tt}_{tb}_{dh}")
                        B.op("v", (lambda bk_, bpp, dd: (lambda eng: eng.tensor_copy(
                                 outsb[:, bpp * 1024 + dd * 512: bpp * 1024 + dd * 512 + 512],
                                 ps[bk_][:, :])))(bank, bp, dh),
                             reads=(f"ps{bank}",), writes=(f"ob{bp}",),
                             tag=f"obe_{ex}_{hh}_{tt}_{tb}_{dh}")
                    bp = blk % 2
                    if hh == 0:
                        B.op("s", (lambda bb_, e_, bpp: (lambda eng: eng.dma_start(
                                 oe_d[e_ * CS + bb_ * 128: e_ * CS + (bb_ + 1) * 128, :],
                                 outsb[:, bpp * 1024:(bpp + 1) * 1024])))(blk, ex, bp),
                             reads=(f"ob{bp}",), writes=(f"od{ex}_{blk}",), sem="dh0",
                             amt=16, tag=f"ost_{ex}_{hh}_{blk}")
                    else:
                        B.op("g", (lambda bb_, e_, bpp: (lambda eng: eng.dma_start(
                                 oe_d[e_ * CS + bb_ * 128: e_ * CS + (bb_ + 1) * 128, :],
                                 outsb[:, bpp * 1024:(bpp + 1) * 1024],
                                 accum_op=ALU.add)))(blk, ex, bp),
                             reads=(f"ob{bp}",), writes=(f"od{ex}_{blk}",), sem="dh1",
                             amt=16, tag=f"ost_{ex}_{hh}_{blk}")

        def emit_C(si):
            base = slot_base(si)
            sk = slot_key(si)

            for c in range(8):
                for dh in range(2):
                    bank = (c * 2 + dh) % 4
                    for sub in range(2):
                        B.op("p", (lambda bk_, cc, dd, ss: (lambda eng: eng.matmul(
                                 ps[bk_][ss * 64:(ss + 1) * 64, :],
                                 spsb[ss * 64:(ss + 1) * 64, cc * 64:(cc + 1) * 64],
                                 wc(base + 8192 + cc * 1024 + dd * 512, 512)[ss * 64:(ss + 1) * 64, :],
                                 start=True, stop=True)))(bank, c, dh, sub),
                             reads=("spsb", sk), writes=(f"ps{bank}",),
                             sem=("sp" if sub == 1 else None),
                             tag=f"T_{c}_{dh}_{sub}")
                    B.op("a", (lambda bk_, cc, dd: (lambda eng: eng.activation(
                             wc(base + 8192 + cc * 1024 + dd * 512, 512),
                             ps[bk_][:, :], AF.Copy)))(bank, c, dh),
                         reads=(f"ps{bank}",), writes=(sk,), tag=f"Te_{c}_{dh}")

            for gt in range(NTT):
                xnfk = tuple(f"xnf{bb}" for bb in range(gt * 4, gt * 4 + 4))
                for co in range(8):
                    bank = co % 4
                    for c in range(8):
                        B.op("p", (lambda bk_, cc, oo, g_: (lambda eng: eng.matmul(
                                 ps[bk_][:, :],
                                 wc(base + cc * 1024 + oo * 128, 128),
                                 xnf[:, cc * TOK + g_ * 512: cc * TOK + (g_ + 1) * 512],
                                 start=(cc == 0), stop=False)))(bank, c, co, gt),
                             reads=(sk,) + xnfk, writes=(f"ps{bank}",), sem=None,
                             tag=f"q_{gt}_{co}_{c}")
                    B.op("p", (lambda bk_, oo: (lambda eng: eng.matmul(
                             ps[bk_][:, :], crow[0:1, oo * 128:(oo + 1) * 128],
                             ones_s[0:1, 0:512], start=False, stop=True)))(bank, co),
                         reads=("crow", "ones"), writes=(f"ps{bank}",), sem="sp",
                         tag=f"qb_{gt}_{co}")
                    B.op("a", (lambda bk_, oo: (lambda eng: eng.activation(
                             hidq[:, oo * 512:(oo + 1) * 512], ps[bk_][:, :],
                             AF.Copy)))(bank, co),
                         reads=(f"ps{bank}",), writes=(f"hid{co}",),
                         tag=f"qe_{gt}_{co}")

                for tb in range(4):
                    bi = gt * 4 + tb
                    p = bi % 2
                    xk = f"xin{p}"
                    if tb == 0 and gt == 0:
                        for j in (0, 1):
                            dma("s", f"dx{j % 2}",
                                xin[:, (j % 2) * 1024:(j % 2) * 1024 + 1024],
                                x_d[j * 128:(j + 1) * 128, :], (), (f"xin{j % 2}",),
                                f"ld_xc{j}")
                    for s in range(3):
                        B.op("g", (lambda ii, ss, pp: (lambda eng: eng.indirect_dma_start(
                                 xef[:, pp * 4096 + ss * 1024: pp * 4096 + (ss + 1) * 1024],
                                 None, oe_d[:, :],
                                 IndirectOffsetOnAxis(
                                     dsti[:, ii * 3 + ss:ii * 3 + ss + 1], 0))))(bi, s, p),
                             reads=("dsti", "odz", f"gth{p}")
                                   + tuple(f"od{e_}_{b_}" for e_ in range(NE)
                                           for b_ in range(8)),
                             writes=(f"gth{p}",),
                             sem=("dg0" if p == 0 else "dg1"), amt=16,
                             tag=f"gth_{bi}_{s}")
                    B.op("v", (lambda ii, pp: (lambda eng: eng.tensor_scalar(
                             acct[:, pp * 1024:(pp + 1) * 1024],
                             xef[:, pp * 4096: pp * 4096 + 1024],
                             gsl[:, ii * 3:ii * 3 + 1], None, op0=ALU.mult)))(bi, p),
                         reads=(f"gth{p}", "gsl"), writes=(f"acct{p}",),
                         tag=f"cmb0_{bi}")
                    for s in (1, 2):
                        B.op("v", (lambda ii, ss, pp: (lambda eng:
                                 eng.scalar_tensor_tensor(
                                     acct[:, pp * 1024:(pp + 1) * 1024],
                                     xef[:, pp * 4096 + ss * 1024: pp * 4096 + (ss + 1) * 1024],
                                     gsl[:, ii * 3 + ss:ii * 3 + ss + 1],
                                     acct[:, pp * 1024:(pp + 1) * 1024],
                                     op0=ALU.mult, op1=ALU.add)))(bi, s, p),
                             reads=(f"gth{p}", "gsl", f"acct{p}"), writes=(f"acct{p}",),
                             tag=f"cmb{s}_{bi}")
                    B.op("a", (lambda pp: (lambda eng: eng.activation(
                             silb[:, pp * 1024:(pp + 1) * 1024],
                             acct[:, pp * 1024:(pp + 1) * 1024], AF.Silu)))(p),
                         reads=(f"acct{p}",), writes=(f"silb{p}",), tag=f"sil_{bi}")
                    for dh in range(2):
                        bank = 4 + (tb * 2 + dh + 2) % 4
                        for c in range(8):
                            B.op("p", (lambda bk_, cc, t_, dd: (lambda eng: eng.matmul(
                                     ps[bk_][:, :],
                                     hidq[:, cc * 512 + t_ * 128: cc * 512 + (t_ + 1) * 128],
                                     wc(base + 8192 + cc * 1024 + dd * 512, 512),
                                     start=(cc == 0), stop=False)))(bank, c, tb, dh),
                                 reads=(f"hid{c}", sk), writes=(f"ps{bank}",),
                                 sem=None, tag=f"at_{bi}_{dh}_{c}")
                        B.op("p", (lambda bk_, dd: (lambda eng: eng.matmul(
                                 ps[bk_][:, :], ones_s[0:1, 0:128],
                                 crow[0:1, 1024 + dd * 512: 1024 + (dd + 1) * 512],
                                 start=False, stop=True)))(bank, dh),
                             reads=("ones", "crow"), writes=(f"ps{bank}",), sem="sp",
                             tag=f"atb_{bi}_{dh}")
                        B.op("v", (lambda bk_, pp, dd: (lambda eng: eng.tensor_mul(
                                 outt[:, pp * 1024 + dd * 512: pp * 1024 + dd * 512 + 512],
                                 ps[bk_][:, :],
                                 silb[:, pp * 1024 + dd * 512: pp * 1024 + dd * 512 + 512])))(bank, p, dh),
                             reads=(f"ps{bank}", f"silb{p}"), writes=(f"out{p}",),
                             tag=f"om_{bi}_{dh}")
                    B.op("v", (lambda pp: (lambda eng: eng.tensor_add(
                             outt[:, pp * 1024:(pp + 1) * 1024],
                             outt[:, pp * 1024:(pp + 1) * 1024],
                             xin[:, pp * 1024:(pp + 1) * 1024])))(p),
                         reads=(f"out{p}", xk), writes=(f"out{p}",),
                         tag=f"oa_{bi}")
                    dma("s", f"dy{p}", y_d[bi * 128:(bi + 1) * 128, :],
                        outt[:, p * 1024:(p + 1) * 1024], (f"out{p}",), (),
                        f"st_y_{bi}")
                    j = bi + 2
                    if j < NB:
                        dma("s", f"dx{j % 2}",
                            xin[:, (j % 2) * 1024:(j % 2) * 1024 + 1024],
                            x_d[j * 128:(j + 1) * 128, :], (), (f"xin{j % 2}",),
                            f"ld_xc{j}")

        for si, st in enumerate(steps):
            if st[0] == "A":
                emit_A1(si)
                emit_base()
                emit_A2(si)
            elif st[0] == "B":
                emit_B(si)
            else:
                emit_C(si)
            if si + 1 < len(steps) and steps[si + 1][0] == "B" and steps[si + 1][2] == 0:
                emit_xe(steps[si + 1][1])
            if si + 2 < len(steps):
                emit_loads(si + 2)

        tot0, tot1 = B.cnt.get("dy0", 0), B.cnt.get("dy1", 0)
        B.finish()

        items = B.items

        def run_engine(eng_obj, ename):
            for (en, wlist, fn, sem, amt, tag) in items:
                if en != ename:
                    continue
                for (s, v) in wlist:
                    eng_obj.wait_ge(sems[s], v)
                inst = fn(eng_obj)
                if sem is not None:
                    inst.then_inc(sems[sem], amt)
            if ename == "s":
                eng_obj.wait_ge(sems["dy0"], tot0)
                eng_obj.wait_ge(sems["dy1"], tot1)

        with nc.Block() as block:
            @block.tensor
            def _(eng):
                run_engine(eng, "p")

            @block.vector
            def _(eng):
                run_engine(eng, "v")

            @block.scalar
            def _(eng):
                run_engine(eng, "a")

            @block.gpsimd
            def _(eng):
                run_engine(eng, "g")

            @block.sync
            def _(eng):
                run_engine(eng, "s")

    return nc


_NC_CACHE = None


def kernel(L0_attn_x, L0_attn_Wq, L0_attn_bq, L0_attn_Wk, L0_attn_bk,
           L0_attn_Wv, L0_attn_bv, L0_attn_Wo, L0_attn_bo,
           L0_ffn_gate, L0_ffn_W1, L0_ffn_b1, L0_ffn_W2, L0_ffn_b2):
    global _NC_CACHE
    if _NC_CACHE is None:
        _NC_CACHE = build_nc()
    nc = _NC_CACHE

    x = np.ascontiguousarray(np.asarray(L0_attn_x, dtype=np.float32))
    shared = {
        "wq": np.ascontiguousarray(L0_attn_Wq, dtype=np.float32),
        "wk": np.ascontiguousarray(L0_attn_Wk, dtype=np.float32),
        "wv": np.ascontiguousarray(L0_attn_Wv, dtype=np.float32),
        "wo": np.ascontiguousarray(L0_attn_Wo, dtype=np.float32),
        "bq": np.ascontiguousarray(L0_attn_bq, dtype=np.float32),
        "bk": np.ascontiguousarray(L0_attn_bk, dtype=np.float32),
        "bv": np.ascontiguousarray(L0_attn_bv, dtype=np.float32),
        "bo": np.ascontiguousarray(L0_attn_bo, dtype=np.float32),
        "gate": np.ascontiguousarray(L0_ffn_gate, dtype=np.float32),
        "w1": np.ascontiguousarray(L0_ffn_W1, dtype=np.float32),
        "b1": np.ascontiguousarray(L0_ffn_b1, dtype=np.float32),
        "w2": np.ascontiguousarray(L0_ffn_W2, dtype=np.float32),
        "b2": np.ascontiguousarray(L0_ffn_b2, dtype=np.float32),
    }
    in_maps = [dict(shared, x=np.ascontiguousarray(x[c])) for c in range(8)]
    res = run_bass_kernel_spmd(nc, in_maps, core_ids=list(range(8)))
    return np.stack([np.asarray(r["y"], dtype=np.float32) for r in res.results])
